# version 03
# baseline (speedup 1.0000x reference)
import numpy as np
from contextlib import ExitStack

import concourse.bass as bass
import concourse.mybir as mybir
from concourse.bass_utils import run_bass_kernel_spmd

F32 = mybir.dt.float32
BF16 = mybir.dt.bfloat16
AF = mybir.ActivationFunctionType
ALU = mybir.AluOpType
AX = mybir.AxisListType

NCORES = 8
B, L, D = 2, 8192, 1024
TPC = (B * L) // NCORES
NT = TPC // 128


class _Op:
    __slots__ = ("stream", "idx", "emit", "clock", "waits", "gid")

    def key(self):
        st = self.stream
        if st.inc == 16:
            return f"{st.name}:{(self.idx - 1) % st.npool}"
        return st.name

    def cnt(self):
        st = self.stream
        if st.inc == 16:
            return ((self.idx - 1) // st.npool + 1) * 16
        return self.idx


class _Stream:
    def __init__(self, name, inc, npool=1):
        self.name = name
        self.inc = inc
        self.npool = npool
        self.ops = []
        self.sems = []

    def sem_for(self, key):
        return self.sems[int(key.split(":")[1])] if ":" in key else self.sems[0]


class Sched:
    NPOOL = 10

    def __init__(self):
        self.streams = {n: _Stream(n, 1) for n in ("pe", "act", "dve", "pool")}
        for n in ("dsp", "dact", "dpool"):
            self.streams[n] = _Stream(n, 16, self.NPOOL)
        self.bufs = {}
        self._gid = 0

    def op(self, stream, emit, reads=(), writes=(), deps=()):
        st = self.streams[stream]
        o = _Op()
        o.stream, o.idx, o.emit = st, len(st.ops) + 1, emit
        self._gid += 1
        o.gid = self._gid
        dep_ops = list(deps) + list(getattr(self, "_barrier", ()))
        if st.inc == 16 and o.idx > st.npool:
            dep_ops.append(st.ops[o.idx - st.npool - 1])
        for b in reads:
            ent = self.bufs.setdefault(b, [None, []])
            if ent[0] is not None:
                dep_ops.append(ent[0])
            ent[1].append(o)
        for b in writes:
            ent = self.bufs.setdefault(b, [None, []])
            if ent[0] is not None:
                dep_ops.append(ent[0])
            dep_ops.extend(ent[1])
            self.bufs[b] = [o, []]
        clk = dict(st.ops[-1].clock) if st.ops else {}
        need = {}
        for d in dep_ops:
            if d is o:
                continue
            for k, v in d.clock.items():
                if clk.get(k, 0) < v:
                    clk[k] = v
            dk, dv = d.key(), d.cnt()
            if clk.get(dk, 0) < dv:
                clk[dk] = dv
            if d.stream is st and st.name == "pe":
                continue
            if need.get(dk, 0) < dv:
                need[dk] = dv
        o.clock = clk
        prev = st.ops[-1].clock if st.ops else {}
        o.waits = {k: v for k, v in need.items() if prev.get(k, 0) < v}
        st.ops.append(o)
        return o

    def _merge(self, *names):
        ops = [o for n in names for o in self.streams[n].ops]
        ops.sort(key=lambda o: o.gid)
        return ops

    def lower(self, nc, es):
        for s in self.streams.values():
            n = s.npool if s.inc == 16 else 1
            s.sems = [es.enter_context(nc.semaphore(f"sem_{s.name}{i}"))
                      for i in range(n)]
        blk = es.enter_context(nc.Block())
        sched = self

        def emit_ops(engine_ops, engine):
            for o in engine_ops:
                for k, v in sorted(o.waits.items()):
                    stn = k.split(":")[0]
                    st = sched.streams[stn]
                    engine.wait_ge(st.sem_for(k), v)
                sem = o.stream.sem_for(o.key())
                o.emit(engine).then_inc(sem, o.stream.inc)

        @blk.tensor
        def _(e):
            emit_ops(self._merge("pe"), e)

        @blk.vector
        def _(e):
            emit_ops(self._merge("dve"), e)

        @blk.gpsimd
        def _(e):
            emit_ops(self._merge("pool", "dpool"), e)

        @blk.scalar
        def _(e):
            emit_ops(self._merge("act", "dact"), e)

        @blk.sync
        def _(e):
            emit_ops(self._merge("dsp"), e)
            for s in self.streams.values():
                for i, sem in enumerate(s.sems):
                    if s.inc == 16:
                        nuse = (len(s.ops) - i + s.npool - 1) // s.npool
                        if nuse:
                            e.wait_ge(sem, nuse * 16)
                    elif s.ops:
                        e.wait_ge(sem, len(s.ops))


class Consts:

    def __init__(self, sch, nc, es, values):
        self.vals = list(dict.fromkeys(values))
        self.tile = es.enter_context(
            nc.sbuf_tensor("consts", [128, max(1, len(self.vals))], F32))
        for i, v in enumerate(self.vals):
            sch.op("pool", lambda e, i=i, v=v: e.memset(self.tile[:, i:i + 1], v),
                   writes=["consts"])

    def col(self, v):
        i = self.vals.index(v)
        return self.tile[:, i:i + 1]


def build_ident(sch, idx_f32, ident_bf):
    sch.op("pool", lambda e: e.iota(idx_f32[:], pattern=[[1, 128]], base=0,
                                    channel_multiplier=-1,
                                    allow_small_or_imprecise_dtypes=True),
           writes=["idx_f32"])
    sch.op("dve", lambda e: e.tensor_scalar(out=ident_bf[:], in0=idx_f32[:],
                                            scalar1=0.0, scalar2=None,
                                            op0=ALU.is_equal),
           reads=["idx_f32"], writes=["ident"])


def ln_stats(sch, x_ap, xbuf, st_ap, stbuf, junk_ap, junkbuf, consts):
    F = x_ap.shape[-1]
    sch.op("dve", lambda e: e.tensor_reduce(out=st_ap[:, 0:1], in_=x_ap,
                                            axis=AX.X, op=ALU.add),
           reads=[xbuf], writes=[stbuf])
    sch.op("dve", lambda e: e.scalar_tensor_tensor(out=junk_ap, in0=x_ap,
                                                   scalar=1.0, in1=x_ap,
                                                   op0=ALU.mult, op1=ALU.mult,
                                                   accum_out=st_ap[:, 1:2]),
           reads=[xbuf], writes=[junkbuf, stbuf])
    sch.op("dve", lambda e: e.tensor_scalar(out=st_ap[:, 4:5], in0=st_ap[:, 0:1],
                                            scalar1=1.0 / F, scalar2=None,
                                            op0=ALU.mult),
           reads=[stbuf], writes=[stbuf])
    sch.op("dve", lambda e: e.tensor_scalar(
        out=st_ap[:, 5:6], in0=st_ap[:, 4:5], scalar1=st_ap[:, 4:5],
        scalar2=None, op0=ALU.mult),
           reads=[stbuf], writes=[stbuf])
    sch.op("dve", lambda e: e.scalar_tensor_tensor(
        out=st_ap[:, 5:6], in0=st_ap[:, 1:2], scalar=1.0 / F,
        in1=st_ap[:, 5:6], op0=ALU.mult, op1=ALU.subtract),
           reads=[stbuf], writes=[stbuf])
    sch.op("act", lambda e: e.activation(out=st_ap[:, 2:3], in_=st_ap[:, 5:6],
                                         func=AF.Ln, bias=consts.col(1e-5),
                                         scale=1.0),
           reads=[stbuf, "consts"], writes=[stbuf])
    sch.op("act", lambda e: e.activation(out=st_ap[:, 2:3], in_=st_ap[:, 2:3],
                                         func=AF.Exp, scale=-0.5,
                                         bias=consts.col(0.0)),
           reads=[stbuf, "consts"], writes=[stbuf])
    sch.op("dve", lambda e: e.tensor_scalar(out=st_ap[:, 3:4], in0=st_ap[:, 4:5],
                                            scalar1=st_ap[:, 2:3], scalar2=-1.0,
                                            op0=ALU.mult, op1=ALU.mult),
           reads=[stbuf], writes=[stbuf])


def ln_apply(sch, x_ap, xbuf, st_ap, stbuf, xn_ap, xnbuf):
    sch.op("dve", lambda e: e.tensor_scalar(out=xn_ap, in0=x_ap,
                                            scalar1=st_ap[:, 2:3],
                                            scalar2=st_ap[:, 3:4],
                                            op0=ALU.mult, op1=ALU.add),
           reads=[xbuf, stbuf], writes=[xnbuf])


def transpose_plane(sch, src_ap, srcbuf, psum_ap, psumbuf, dst_ap, dstbuf,
                    ident_bf, nchunk):
    for c in range(nchunk):
        cc = c
        sch.op("pe", lambda e, cc=cc: e.transpose(
            psum_ap[:, cc * 128:(cc + 1) * 128],
            src_ap[:, cc * 128:(cc + 1) * 128], ident_bf[:]),
               reads=[srcbuf, "ident"], writes=[psumbuf])
    sch.op("dve", lambda e: e.tensor_copy(dst_ap, psum_ap[:]),
           reads=[psumbuf], writes=[dstbuf])


def mm_chain(sch, psum_ap, psumbuf, lhs_ap, lhsbuf, rhs_ap, rhsbuf, nchunk,
             n0=0, n1=None, extra=(), extra_reads=()):
    for c in range(nchunk):
        cc = c
        last = (cc == nchunk - 1) and not extra
        sch.op("pe", lambda e, cc=cc, last=last: e.matmul(
            psum_ap, lhs_ap[:, cc * 128:(cc + 1) * 128], rhs_ap(cc),
            start=(cc == 0), stop=last),
               reads=[lhsbuf, rhsbuf], writes=[psumbuf])
    for i, (la, ra) in enumerate(extra):
        last = i == len(extra) - 1
        sch.op("pe", lambda e, la=la, ra=ra, last=last: e.matmul(
            psum_ap, la, ra, start=False, stop=last),
               reads=list(extra_reads) + [lhsbuf], writes=[psumbuf])


def prep_wT(sch, nc, es, w_ap, R, C, dst, dstbuf, ident_bf, tag, pt, ptn):
    nR, nC = R // 128, C // 128
    ring = [es.enter_context(nc.sbuf_tensor(f"wp_{tag}{i}", [128, C], BF16))
            for i in range(2)]
    for rr in range(nR):
        sl = rr % 2
        sch.op("dpool", lambda e, rr=rr, sl=sl: e.dma_start(
            out=ring[sl][:], in_=w_ap[rr * 128:(rr + 1) * 128, :]),
               writes=[f"wp_{tag}{sl}"])
        for cg in range((nC + 7) // 8):
            c0, c1 = cg * 8, min(nC, cg * 8 + 8)
            psl = (rr * ((nC + 7) // 8) + cg) % 2
            for c in range(c0, c1):
                sch.op("pe", lambda e, c=c, c0=c0, sl=sl, psl=psl: e.transpose(
                    pt[psl][:, (c - c0) * 128:(c - c0 + 1) * 128],
                    ring[sl][:, c * 128:(c + 1) * 128], ident_bf[:]),
                       reads=[f"wp_{tag}{sl}", "ident"],
                       writes=[ptn[psl]])
            n = c1 - c0
            sch.op("dve", lambda e, c0=c0, n=n, rr=rr, psl=psl: e.tensor_copy(
                dst[:, c0:c0 + n, rr * 128:(rr + 1) * 128],
                pt[psl][:, 0:n * 128].rearrange("p (c t) -> p c t", c=n)),
                   reads=[ptn[psl]], writes=[dstbuf])


def build_l1ffn(sch, nc, es, xin_ap, wfc_ap, wproj_ap, xout_ap, ntiles=NT):
    sb = lambda name, shape, dt=F32: es.enter_context(nc.sbuf_tensor(name, shape, dt))
    ps = lambda name, shape, dt=F32: es.enter_context(nc.psum_tensor(name, shape, dt))

    idx_f32 = sb("idx_f32", [128, 128])
    ident = sb("ident", [128, 128], BF16)
    build_ident(sch, idx_f32, ident)
    consts = Consts(sch, nc, es, [0.0, 1e-5])

    pbf = [ps(f"pbf{i}", [128, 1024], BF16) for i in range(2)]
    pbfn = ["pbf0", "pbf1"]
    pf = [ps(f"pf{i}", [128, 512]) for i in range(2)]

    wfcT = sb("wfcT", [128, 8, 4096], BF16)
    wprojT = sb("wprojT", [128, 32, 1024], BF16)
    prep_wT(sch, nc, es, wfc_ap, 4096, 1024, wfcT, "wfcT", ident, "fc", pbf, pbfn)
    prep_wT(sch, nc, es, wproj_ap, 1024, 4096, wprojT, "wprojT", ident, "pj", pbf, pbfn)

    NSL = 2
    xt = [sb(f"xt{i}", [128, 1024]) for i in range(NSL)]
    stt_ = [sb(f"st{i}", [128, 8]) for i in range(NSL)]
    junk = sb("junk", [128, 1024])
    xn = [sb(f"xn{i}", [128, 1024], BF16) for i in range(NSL)]
    pT = pbf
    xnf = [sb(f"xnf{i}", [128, 1024], BF16) for i in range(NSL)]
    ph = pf
    ru = [sb(f"ru{i}", [128, 512], BF16) for i in range(2)]
    h2 = sb("h2", [128, 4096], BF16)
    pH = pbf
    h2f = [sb(f"h2f{i}", [128, 1024], BF16) for i in range(2)]
    py = pf
    ys = sb("ys", [128, 1024])
    mem = sb("mem", [128, 1024])
    acc = sb("acc", [128, 1024])
    spk = sb("spk", [128, 1024])

    for t in range(ntiles):
        s = t % NSL
        T = lambda n: f"{n}{s}"
        sch.op("dsp", lambda e, t=t, s=s: e.dma_start(
            out=xt[s][:], in_=xin_ap[t * 128:(t + 1) * 128, :]),
            writes=[T("xt")])
        ln_stats(sch, xt[s][:], T("xt"), stt_[s], T("st"), junk[:], "junk", consts)
        ln_apply(sch, xt[s][:], T("xt"), stt_[s], T("st"), xn[s][:], T("xn"))
        transpose_plane(sch, xn[s][:], T("xn"), pT[s], pbfn[s], xnf[s][:],
                        T("xnf"), ident, 8)
        for fo in range(8):
            pp = fo % 2
            mm_chain(sch, ph[pp][:], f"pf{pp}", xnf[s][:], T("xnf"),
                     lambda c, fo=fo: wfcT[:, c, fo * 512:(fo + 1) * 512],
                     "wfcT", 8)
            sch.op("act", lambda e, fo=fo, pp=pp: e.activation(
                out=ru[pp][:], in_=ph[pp][:], func=AF.Relu,
                bias=consts.col(0.0), scale=1.0),
                reads=[f"pf{pp}", "consts"], writes=[f"ru{pp}"])
            sch.op("pool", lambda e, fo=fo, pp=pp: e.tensor_tensor(
                out=h2[:, fo * 512:(fo + 1) * 512],
                in0=ru[pp][:], in1=ru[pp][:], op=ALU.mult),
                reads=[f"ru{pp}"], writes=["h2"])
        for g in range(4):
            pp = g % 2
            for c in range(8):
                sch.op("pe", lambda e, g=g, c=c, pp=pp: e.transpose(
                    pH[pp][:, c * 128:(c + 1) * 128],
                    h2[:, (g * 8 + c) * 128:(g * 8 + c + 1) * 128],
                    ident[:]),
                    reads=["h2", "ident"], writes=[pbfn[pp]])
            sch.op("dve", lambda e, g=g, pp=pp: e.tensor_copy(
                h2f[pp][:], pH[pp][:]),
                reads=[pbfn[pp]], writes=[f"h2f{pp}"])
            for oo in range(2):
                for c in range(8):
                    k = g * 8 + c
                    sch.op("pe", lambda e, g=g, c=c, oo=oo, pp=pp, k=k: e.matmul(
                        py[oo][:], h2f[pp][:, c * 128:(c + 1) * 128],
                        wprojT[:, k, oo * 512:(oo + 1) * 512],
                        start=(k == 0), stop=(k == 31)),
                        reads=[f"h2f{pp}", "wprojT"], writes=[f"pf{oo}"])
        for oo in range(2):
            sch.op("dve", lambda e, oo=oo: e.tensor_scalar(
                out=ys[:, oo * 512:(oo + 1) * 512], in0=py[oo][:],
                scalar1=1.0, scalar2=None, op0=ALU.mult),
                reads=[f"pf{oo}"], writes=["ys"])
        sch.op("dve", lambda e: e.tensor_scalar(
            out=acc[:], in0=ys[:], scalar1=1.0, scalar2=None,
            op0=ALU.is_gt), reads=["ys"], writes=["acc"])
        sch.op("dve", lambda e: e.tensor_copy(mem[:], ys[:]),
               reads=["ys"], writes=["mem"])
        sch.op("dve", lambda e: e.tensor_copy(spk[:], acc[:]),
               reads=["acc"], writes=["spk"])
        for it in range(4):
            sch.op("dve", lambda e: e.scalar_tensor_tensor(
                out=mem[:], in0=mem[:], scalar=0.9, in1=ys[:],
                op0=ALU.mult, op1=ALU.add),
                reads=["mem", "ys"], writes=["mem"])
            sch.op("dve", lambda e: e.tensor_tensor(
                out=mem[:], in0=mem[:], in1=spk[:], op=ALU.subtract),
                reads=["mem", "spk"], writes=["mem"])
            sch.op("dve", lambda e: e.tensor_scalar(
                out=spk[:], in0=mem[:], scalar1=1.0, scalar2=None,
                op0=ALU.is_gt), reads=["mem"], writes=["spk"])
            sch.op("dve", lambda e: e.tensor_tensor(
                out=acc[:], in0=acc[:], in1=spk[:], op=ALU.add),
                reads=["acc", "spk"], writes=["acc"])
        sch.op("dve", lambda e, s=s: e.scalar_tensor_tensor(
            out=acc[:], in0=acc[:], scalar=0.2, in1=xt[s][:],
            op0=ALU.mult, op1=ALU.add),
            reads=["acc", T("xt")], writes=["acc"])
        sch.op("dsp", lambda e, t=t: e.dma_start(
            out=xout_ap[t * 128:(t + 1) * 128, :], in_=acc[:]),
            reads=["acc"], writes=["acc_st"])


def make_l1ffn_program():
    nc = bass.Bass("TRN2", target_bir_lowering=False)
    xin = nc.dram_tensor("xin", [TPC, D], F32, kind="ExternalInput").ap()
    wfc = nc.dram_tensor("wfc", [4096, 1024], F32, kind="ExternalInput").ap()
    wproj = nc.dram_tensor("wproj", [1024, 4096], F32, kind="ExternalInput").ap()
    xout = nc.dram_tensor("xout", [TPC, D], F32, kind="ExternalOutput").ap()
    es = ExitStack()
    with es:
        sch = Sched()
        build_l1ffn(sch, nc, es, xin, wfc, wproj, xout)
        sch.lower(nc, es)
    return nc


PI = float(np.pi)
PI2_HI = 6.283203125
PI2_LO = 6.283185307179586 - PI2_HI
MAGIC = 12582912.0
LN_EIGHTH = float(np.log(0.125))


def rope_tables(sch, nc, es, consts, posb, tabs, ntiles, invfreq_rep, tag=""):
    scr = es.enter_context(nc.sbuf_tensor(f"ropescr{tag}", [128, 4, 32], F32))
    S = f"ropescr{tag}"
    for t in range(ntiles):
        sch.op("dve", lambda e, t=t: e.tensor_scalar(
            out=scr[:, 0], in0=invfreq_rep[:], scalar1=posb[:],
            scalar2=None, op0=ALU.mult), reads=["posb", "invfreq"], writes=[S])
        sch.op("dve", lambda e, t=t: e.scalar_tensor_tensor(
            out=scr[:, 0], in0=invfreq_rep[:], scalar=float(128 * t),
            in1=scr[:, 0], op0=ALU.mult, op1=ALU.add),
            reads=["invfreq", S], writes=[S])
        sch.op("dve", lambda e: e.tensor_scalar(
            out=scr[:, 1], in0=scr[:, 0], scalar1=1.0 / (2 * PI),
            scalar2=MAGIC, op0=ALU.mult, op1=ALU.add), reads=[S], writes=[S])
        sch.op("dve", lambda e: e.tensor_scalar(
            out=scr[:, 1], in0=scr[:, 1], scalar1=-MAGIC, scalar2=None,
            op0=ALU.add), reads=[S], writes=[S])
        sch.op("dve", lambda e: e.scalar_tensor_tensor(
            out=scr[:, 2], in0=scr[:, 1], scalar=-PI2_HI, in1=scr[:, 0],
            op0=ALU.mult, op1=ALU.add), reads=[S], writes=[S])
        sch.op("dve", lambda e: e.scalar_tensor_tensor(
            out=scr[:, 2], in0=scr[:, 1], scalar=-PI2_LO, in1=scr[:, 2],
            op0=ALU.mult, op1=ALU.add), reads=[S], writes=[S])
        sch.op("dve", lambda e: e.tensor_scalar(
            out=scr[:, 2], in0=scr[:, 2], scalar1=-PI, scalar2=PI,
            op0=ALU.max, op1=ALU.min), reads=[S], writes=[S])
        sch.op("act", lambda e, t=t: e.activation(
            out=tabs[:, t, 1], in_=scr[:, 2], func=AF.Sin,
            bias=consts.col(0.0), scale=1.0), reads=[S, "consts"],
            writes=["tabs"])
        sch.op("dve", lambda e: e.tensor_scalar(
            out=scr[:, 3], in0=scr[:, 2], scalar1=PI / 2, scalar2=None,
            op0=ALU.add), reads=[S], writes=[S])
        sch.op("dve", lambda e: e.tensor_scalar(
            out=scr[:, 1], in0=scr[:, 3], scalar1=PI, scalar2=None,
            op0=ALU.is_gt), reads=[S], writes=[S])
        sch.op("dve", lambda e: e.scalar_tensor_tensor(
            out=scr[:, 3], in0=scr[:, 1], scalar=-2 * PI, in1=scr[:, 3],
            op0=ALU.mult, op1=ALU.add), reads=[S], writes=[S])
        sch.op("dve", lambda e: e.tensor_scalar(
            out=scr[:, 3], in0=scr[:, 3], scalar1=-PI, scalar2=PI,
            op0=ALU.max, op1=ALU.min), reads=[S], writes=[S])
        sch.op("act", lambda e, t=t: e.activation(
            out=tabs[:, t, 0], in_=scr[:, 3], func=AF.Sin,
            bias=consts.col(0.0), scale=1.0), reads=[S, "consts"],
            writes=["tabs"])


def build_l0attn(sch, nc, es, xh_ap, wqkv_ap, wout_ap, kb_ap, pos0_ap, xout_ap,
                 nq=NT):
    sb = lambda name, shape, dt=F32: es.enter_context(nc.sbuf_tensor(name, shape, dt))
    ps = lambda name, shape, dt=F32: es.enter_context(nc.psum_tensor(name, shape, dt))
    ngt = nq + 4

    idx_f32 = sb("idx_f32", [128, 128])
    ident = sb("ident", [128, 128], BF16)
    build_ident(sch, idx_f32, ident)
    consts = Consts(sch, nc, es, [0.0, 1e-5])

    far01 = sb("far01", [128, 128], BF16)
    diag01 = sb("diag01", [128, 128], BF16)
    sch.op("dve", lambda e: e.tensor_scalar(out=far01[:], in0=idx_f32[:],
                                            scalar1=0.0, scalar2=None,
                                            op0=ALU.is_lt),
           reads=["idx_f32"], writes=["far01"])
    sch.op("dve", lambda e: e.tensor_scalar(out=diag01[:], in0=idx_f32[:],
                                            scalar1=0.0, scalar2=None,
                                            op0=ALU.is_ge),
           reads=["idx_f32"], writes=["diag01"])

    wqkv_sb = sb("wqkv_sb", [128, 8, 3072], BF16)
    sch.op("dpool", lambda e: e.dma_start(
        out=wqkv_sb[:], in_=wqkv_ap.rearrange("(c p) n -> p c n", p=128)),
        writes=["wqkv_sb"])
    wout_sb = sb("wout_sb", [128, 8, 1024], BF16)
    sch.op("dpool", lambda e: e.dma_start(
        out=wout_sb[:], in_=wout_ap.rearrange("(c p) n -> p c n", p=128)),
        writes=["wout_sb"])

    ones_f = sb("ones_f", [1, 128], F32)
    sch.op("pool", lambda e: e.memset(ones_f[:], 1.0), writes=["ones_f"])
    pos0_sb = sb("pos0_sb", [1, 1], F32)
    sch.op("dsp", lambda e: e.dma_start(out=pos0_sb[:], in_=pos0_ap),
           writes=["pos0_sb"])
    pmisc = ps("pmisc", [128, 128], F32)
    sch.op("pe", lambda e: e.matmul(pmisc[:, 0:1], ones_f[:], pos0_sb[:],
                                    start=True, stop=True),
           reads=["ones_f", "pos0_sb"], writes=["pmisc"])
    posb = sb("posb", [128, 1], F32)
    rowi = sb("rowi", [128, 1], F32)
    sch.op("pool", lambda e: e.iota(rowi[:], pattern=[[0, 1]], base=0,
                                    channel_multiplier=1,
                                    allow_small_or_imprecise_dtypes=True),
           writes=["rowi"])
    sch.op("dve", lambda e: e.scalar_tensor_tensor(
        out=posb[:], in0=pmisc[:, 0:1], scalar=1.0, in1=rowi[:],
        op0=ALU.mult, op1=ALU.add), reads=["pmisc", "rowi"], writes=["posb"])

    invfreq = sb("invfreq", [128, 32], F32)
    sch.op("pool", lambda e: e.iota(invfreq[:], pattern=[[1, 32]], base=0,
                                    channel_multiplier=0,
                                    allow_small_or_imprecise_dtypes=True),
           writes=["invfreq"])
    sch.op("act", lambda e: e.activation(out=invfreq[:], in_=invfreq[:],
                                         func=AF.Exp,
                                         scale=-float(np.log(10000.0)) / 32,
                                         bias=consts.col(0.0)),
           reads=["invfreq", "consts"], writes=["invfreq"])
    tabs = sb("tabs", [128, ngt, 2, 32], F32)
    rope_tables(sch, nc, es, consts, posb, tabs, ngt, invfreq)

    kb_sb = sb("kb_sb", [128, 4], F32)
    sch.op("dsp", lambda e: e.dma_start(out=kb_sb[:], in_=kb_ap),
           writes=["kb_sb"])

    NSL = 2
    xt = [sb(f"xt{i}", [128, 1024]) for i in range(NSL)]
    stt_ = [sb(f"st{i}", [128, 8]) for i in range(NSL)]
    junk = sb("junk", [128, 1024])
    xn = [sb(f"xn{i}", [128, 1024], BF16) for i in range(NSL)]
    pbf = [ps(f"pbf{i}", [128, 1024], BF16) for i in range(2)]
    pbfn = ["pbf0", "pbf1"]
    xnf = [sb(f"xnf{i}", [128, 1024], BF16) for i in range(NSL)]
    pqk = ps("pqk", [128, 512])
    qtm = sb("qtm", [128, 1024])
    ktm = sb("ktm", [128, 1024])
    rtmp = [sb(f"rtmp{i}", [128, 512]) for i in range(2)]
    qbf = sb("qbf", [128, 1024], BF16)
    kbf = sb("kbf", [128, 1024], BF16)
    qss = sb("qss", [128, 16])
    kss16 = sb("kss16", [128, 16])
    KR = 6
    kfm = sb("kfm", [64, KR, 16, 128], BF16)
    qfm = [sb(f"qfm{i}", [64, 16, 128], BF16) for i in range(NSL)]
    vtm = sb("vtm", [128, KR, 16, 66], BF16)
    sch.op("pool", lambda e: e.memset(vtm[:], 1.0), writes=["vtm"])
    statr = sb("statr", [16, 8])
    qmaxr = sb("qmaxr", [16, KR])
    kmaxr = sb("kmaxr", [16, KR])
    mrow = sb("mrow", [1, 16], F32)
    negm8 = sb("negm8", [128, 16])
    hb = sb("hb", [128, 1])
    identf = sb("identf", [128, 128], F32)
    sch.op("dve", lambda e: e.tensor_copy(identf[:], ident[:]),
           reads=["ident"], writes=["identf"])
    pS = ps("pS", [128, 640])
    attn = sb("attn", [128, 640], BF16)
    pO = ps("pO", [128, 8, 128])
    Rr = sb("Rr", [128, 8])
    otm = sb("otm", [128, 1024], BF16)
    ofm = sb("ofm", [128, 1024], BF16)
    xo = sb("xo", [128, 1024])

    def qkv_tile(gt):
        s = gt % NSL
        slot = gt % KR
        T = lambda n: f"{n}{s}"
        own = gt >= 4
        sch.op("dsp", lambda e, gt=gt, s=s: e.dma_start(
            out=xt[s][:], in_=xh_ap[gt * 128:(gt + 1) * 128, :]),
            writes=[T("xt")])
        ln_stats(sch, xt[s][:], T("xt"), stt_[s], T("st"), junk[:], "junk",
                 consts)
        ln_apply(sch, xt[s][:], T("xt"), stt_[s], T("st"), xn[s][:], T("xn"))
        transpose_plane(sch, xn[s][:], T("xn"), pbf[s], pbfn[s], xnf[s][:],
                        T("xnf"), ident, 8)
        for j in range(6):
            if j < 2 and not own:
                continue
            mm_chain(sch, pqk[:], "pqk", xnf[s][:], T("xnf"),
                     lambda c, j=j: wqkv_sb[:, c, j * 512:(j + 1) * 512],
                     "wqkv_sb", 8)
            if j < 4:
                dst, dbuf = (qtm, "qtm") if j < 2 else (ktm, "ktm")
                jj = j % 2
                sch.op("dve", lambda e, dst=dst, jj=jj: e.tensor_copy(
                    dst[:, jj * 512:(jj + 1) * 512], pqk[:]),
                    reads=["pqk"], writes=[dbuf])
            else:
                h0 = (j - 4) * 8
                sch.op("dve", lambda e, h0=h0, slot=slot: e.tensor_copy(
                    vtm[:, slot, h0:h0 + 8, 0:64],
                    pqk[:].rearrange("p (h d) -> p h d", h=8)),
                    reads=["pqk"], writes=["vtm"])
        for (src, sbuf_, dst, dbuf, do) in (
                (qtm, "qtm", qbf, "qbf", own), (ktm, "ktm", kbf, "kbf", True)):
            if not do:
                continue
            ev = src[:].rearrange("p (h t d) -> p h t d", h=16, t=32)[:, :, :, 0]
            od = src[:].rearrange("p (h t d) -> p h t d", h=16, t=32)[:, :, :, 1]
            dev = dst[:].rearrange("p (h t d) -> p h t d", h=16, t=32)[:, :, :, 0]
            dod = dst[:].rearrange("p (h t d) -> p h t d", h=16, t=32)[:, :, :, 1]
            cosv = tabs[:, gt, 0].unsqueeze(1).broadcast_to([128, 16, 32])
            sinv = tabs[:, gt, 1].unsqueeze(1).broadcast_to([128, 16, 32])
            sch.op("dve", lambda e, od=od, sinv=sinv: e.tensor_tensor(
                out=rtmp[0][:].rearrange("p (h d) -> p h d", h=16),
                in0=od, in1=sinv, op=ALU.mult),
                reads=[sbuf_, "tabs"], writes=["rtmp0"])
            sch.op("dve", lambda e, ev=ev, cosv=cosv: e.tensor_tensor(
                out=rtmp[1][:].rearrange("p (h d) -> p h d", h=16),
                in0=ev, in1=cosv, op=ALU.mult),
                reads=[sbuf_, "tabs"], writes=["rtmp1"])
            sch.op("dve", lambda e, dev=dev: e.tensor_tensor(
                out=dev, in0=rtmp[1][:].rearrange("p (h d) -> p h d", h=16),
                in1=rtmp[0][:].rearrange("p (h d) -> p h d", h=16),
                op=ALU.subtract),
                reads=["rtmp0", "rtmp1"], writes=[dbuf])
            sch.op("dve", lambda e, od=od, cosv=cosv: e.tensor_tensor(
                out=rtmp[0][:].rearrange("p (h d) -> p h d", h=16),
                in0=od, in1=cosv, op=ALU.mult),
                reads=[sbuf_, "tabs"], writes=["rtmp0"])
            sch.op("dve", lambda e, ev=ev, sinv=sinv: e.tensor_tensor(
                out=rtmp[1][:].rearrange("p (h d) -> p h d", h=16),
                in0=ev, in1=sinv, op=ALU.mult),
                reads=[sbuf_, "tabs"], writes=["rtmp1"])
            sch.op("dve", lambda e, dod=dod: e.tensor_tensor(
                out=dod, in0=rtmp[0][:].rearrange("p (h d) -> p h d", h=16),
                in1=rtmp[1][:].rearrange("p (h d) -> p h d", h=16),
                op=ALU.add),
                reads=["rtmp0", "rtmp1"], writes=[dbuf])
        if own:
            sch.op("pool", lambda e: e.tensor_tensor(
                out=junk[:], in0=qbf[:], in1=qbf[:], op=ALU.mult),
                reads=["qbf"], writes=["junk"])
            sch.op("dve", lambda e: e.tensor_reduce(
                out=qss[:], in_=junk[:].rearrange("p (h d) -> p h d", h=16),
                axis=AX.X, op=ALU.add), reads=["junk"], writes=["qss"])
        sch.op("pool", lambda e: e.tensor_tensor(
            out=junk[:], in0=kbf[:], in1=kbf[:], op=ALU.mult),
            reads=["kbf"], writes=["junk"])
        sch.op("dve", lambda e: e.tensor_reduce(
            out=kss16[:], in_=junk[:].rearrange("p (h d) -> p h d", h=16),
            axis=AX.X, op=ALU.add), reads=["junk"], writes=["kss16"])
        if own:
            sch.op("pe", lambda e: e.transpose(pmisc[0:16, :], qss[:], identf[:]),
                   reads=["qss", "identf"], writes=["pmisc"])
            sch.op("dve", lambda e, slot=slot: e.tensor_reduce(
                out=qmaxr[:, slot:slot + 1], in_=pmisc[0:16, :], axis=AX.X,
                op=ALU.max), reads=["pmisc"], writes=["qmaxr"])
        sch.op("pe", lambda e: e.transpose(pmisc[0:16, :], kss16[:], identf[:]),
               reads=["kss16", "identf"], writes=["pmisc"])
        sch.op("dve", lambda e, slot=slot: e.tensor_reduce(
            out=kmaxr[:, slot:slot + 1], in_=pmisc[0:16, :], axis=AX.X,
            op=ALU.max), reads=["pmisc"], writes=["kmaxr"])
        if own:
            for c in range(8):
                sch.op("pe", lambda e, c=c: e.transpose(
                    pbf[s][:, c * 128:(c + 1) * 128],
                    qbf[:, c * 128:(c + 1) * 128], ident[:]),
                    reads=["qbf", "ident"], writes=[pbfn[s]])
            sch.op("dve", lambda e, s=s: e.tensor_copy(
                qfm[s][:, 0:16:2, :],
                pbf[s][0:64, :].rearrange("p (c t) -> p c t", c=8)),
                reads=[pbfn[s]], writes=[T("qfm")])
            sch.op("dve", lambda e, s=s: e.tensor_copy(
                qfm[s][:, 1:16:2, :],
                pbf[s][64:128, :].rearrange("p (c t) -> p c t", c=8)),
                reads=[pbfn[s]], writes=[T("qfm")])
        for c in range(8):
            sch.op("pe", lambda e, c=c, s=s: e.transpose(
                pbf[s][:, c * 128:(c + 1) * 128],
                kbf[:, c * 128:(c + 1) * 128], ident[:]),
                reads=["kbf", "ident"], writes=[pbfn[s]])
        sch.op("dve", lambda e, s=s, slot=slot: e.tensor_copy(
            kfm[:, slot, 0:16:2, :],
            pbf[s][0:64, :].rearrange("p (c t) -> p c t", c=8)),
            reads=[pbfn[s]], writes=["kfm"])
        sch.op("dve", lambda e, s=s, slot=slot: e.tensor_copy(
            kfm[:, slot, 1:16:2, :],
            pbf[s][64:128, :].rearrange("p (c t) -> p c t", c=8)),
            reads=[pbfn[s]], writes=["kfm"])

    def attn_tile(qt):
        gt = qt + 4
        s = gt % NSL
        slot5 = [(gt - 4 + i) % KR for i in range(5)]
        sch.op("dve", lambda e, sl=slot5: e.tensor_copy(
            statr[:, 1:2], kmaxr[:, sl[0]:sl[0] + 1]),
            reads=["kmaxr"], writes=["statr"])
        for i in range(1, 5):
            sch.op("dve", lambda e, sl=slot5, i=i: e.tensor_tensor(
                out=statr[:, 1:2], in0=statr[:, 1:2],
                in1=kmaxr[:, sl[i]:sl[i] + 1], op=ALU.max),
                reads=["statr", "kmaxr"], writes=["statr"])
        sch.op("dve", lambda e, gt=gt: e.tensor_tensor(
            out=statr[:, 2:3], in0=statr[:, 1:2],
            in1=qmaxr[:, gt % KR:gt % KR + 1], op=ALU.mult),
            reads=["statr", "qmaxr"], writes=["statr"])
        sch.op("act", lambda e: e.activation(
            out=statr[:, 3:4], in_=statr[:, 2:3], func=AF.Ln,
            bias=consts.col(1e-5)[0:16, :], scale=1.0),
            reads=["statr", "consts"], writes=["statr"])
        sch.op("act", lambda e: e.activation(
            out=statr[:, 3:4], in_=statr[:, 3:4], func=AF.Exp, scale=0.5,
            bias=lnc8[0:16, :]), reads=["statr", "lnc8"], writes=["statr"])
        sch.op("pe", lambda e: e.transpose(pmisc[0:1, 0:16], statr[:, 3:4],
                                           identf[0:16, 0:16]),
               reads=["statr", "identf"], writes=["pmisc"])
        sch.op("dve", lambda e: e.tensor_copy(mrow[:], pmisc[0:1, 0:16]),
               reads=["pmisc"], writes=["mrow"])
        sch.op("pe", lambda e: e.matmul(pmisc[:, 0:16], ones_f[:], mrow[:],
                                        start=True, stop=True),
               reads=["ones_f", "mrow"], writes=["pmisc"])
        sch.op("dve", lambda e: e.tensor_scalar(
            out=negm8[:], in0=pmisc[:, 0:16], scalar1=-1.0, scalar2=None,
            op0=ALU.mult), reads=["pmisc"], writes=["negm8"])
        for hg in range(2):
            for hh in range(8):
                h = hg * 8 + hh
                for i in range(5):
                    kt = gt - 4 + i
                    sl = slot5[i]
                    sch.op("pe", lambda e, sl=sl, h=h, i=i, s=s: e.matmul(
                        pS[:, i * 128:(i + 1) * 128], kfm[:, sl, h, :],
                        qfm[s][:, h, :], start=True, stop=True),
                        reads=["kfm", f"qfm{s}"], writes=["pS"])
                nhalo = max(0, 4 - qt)
                for i in range(5):
                    kt = gt - 4 + i
                    if i < nhalo:
                        sch.op("dve", lambda e, kt=kt, h=h: e.scalar_tensor_tensor(
                            out=hb[:], in0=kb_sb[:, kt:kt + 1], scalar=0.125,
                            in1=negm8[:, h:h + 1], op0=ALU.mult, op1=ALU.add),
                            reads=["kb_sb", "negm8"], writes=["hb"])
                        sch.op("act", lambda e, i=i, h=h: e.activation(
                            out=attn[:, i * 128:(i + 1) * 128],
                            in_=pS[:, i * 128:(i + 1) * 128], func=AF.Exp,
                            scale=0.125, bias=hb[:]),
                            reads=["pS", "hb"], writes=["attn"])
                if nhalo < 5:
                    sch.op("act", lambda e, nhalo=nhalo, h=h: e.activation(
                        out=attn[:, nhalo * 128:640],
                        in_=pS[:, nhalo * 128:640], func=AF.Exp,
                        scale=0.125, bias=negm8[:, h:h + 1]),
                        reads=["pS", "negm8"], writes=["attn"])
                sch.op("pool", lambda e: e.tensor_tensor(
                    out=attn[:, 0:128], in0=attn[:, 0:128], in1=far01[:],
                    op=ALU.mult), reads=["attn", "far01"], writes=["attn"])
                sch.op("pool", lambda e: e.tensor_tensor(
                    out=attn[:, 512:640], in0=attn[:, 512:640], in1=diag01[:],
                    op=ALU.mult), reads=["attn", "diag01"], writes=["attn"])
                for i in range(5):
                    sl = slot5[i]
                    sch.op("pe", lambda e, sl=sl, h=h, hh=hh, i=i: e.matmul(
                        pO[:, hh, 0:65], attn[:, i * 128:(i + 1) * 128],
                        vtm[:, sl, h, 0:65], start=(i == 0), stop=(i == 4)),
                        reads=["attn", "vtm"], writes=["pO"])
            sch.op("dve", lambda e, hg=hg: e.reciprocal(
                out=Rr[:], in_=pO[:, :, 64:65].squeeze(2)),
                reads=["pO"], writes=["Rr"])
            sch.op("dve", lambda e, hg=hg: e.tensor_tensor(
                out=otm[:].rearrange("p (h d) -> p h d", h=16)[:, hg * 8:(hg + 1) * 8, :],
                in0=pO[:, :, 0:64],
                in1=Rr[:].unsqueeze(2).broadcast_to([128, 8, 64]),
                op=ALU.mult), reads=["pO", "Rr"], writes=["otm"])
        for c in range(8):
            sch.op("pe", lambda e, c=c, s=s: e.transpose(
                pbf[s][:, c * 128:(c + 1) * 128],
                otm[:, c * 128:(c + 1) * 128], ident[:]),
                reads=["otm", "ident"], writes=[pbfn[s]])
        sch.op("dve", lambda e, s=s: e.tensor_copy(ofm[:], pbf[s][:]),
               reads=[pbfn[s]], writes=["ofm"])
        for oc in range(2):
            mm_chain(sch, pqk[:], "pqk", ofm[:], "ofm",
                     lambda c, oc=oc: wout_sb[:, c, oc * 512:(oc + 1) * 512],
                     "wout_sb", 8)
            sch.op("dve", lambda e, oc=oc, s=s: e.scalar_tensor_tensor(
                out=xo[:, oc * 512:(oc + 1) * 512], in0=pqk[:], scalar=1.0,
                in1=xt[s][:, oc * 512:(oc + 1) * 512], op0=ALU.mult,
                op1=ALU.add), reads=["pqk", f"xt{s}"], writes=["xo"])
        sch.op("dsp", lambda e, qt=qt: e.dma_start(
            out=xout_ap[qt * 128:(qt + 1) * 128, :], in_=xo[:]),
            reads=["xo"], writes=["xo_st"])

    lnc8 = sb("lnc8", [128, 1], F32)
    sch.op("pool", lambda e: e.memset(lnc8[:], LN_EIGHTH), writes=["lnc8"])

    for gt in range(4):
        qkv_tile(gt)
    for qt in range(nq):
        qkv_tile(qt + 4)
        attn_tile(qt)


def make_l0attn_program(nq=NT):
    nc = bass.Bass("TRN2", target_bir_lowering=False)
    xh = nc.dram_tensor("xh", [nq * 128 + 512, D], F32, kind="ExternalInput").ap()
    wqkv = nc.dram_tensor("wqkv", [1024, 3072], F32, kind="ExternalInput").ap()
    wout = nc.dram_tensor("wout", [1024, 1024], F32, kind="ExternalInput").ap()
    kb = nc.dram_tensor("kb", [128, 4], F32, kind="ExternalInput").ap()
    pos0 = nc.dram_tensor("pos0", [1, 1], F32, kind="ExternalInput").ap()
    xout = nc.dram_tensor("xout", [nq * 128, D], F32, kind="ExternalOutput").ap()
    es = ExitStack()
    with es:
        sch = Sched()
        build_l0attn(sch, nc, es, xh, wqkv, wout, kb, pos0, xout, nq=nq)
        sch.lower(nc, es)
    return nc


def build_ssm_proj(sch, nc, es, xin_ap, z_ap, win_ap, bin_ap, wgate_ap,
                   bgate_ap, xp_ap, gt_ap, x3_ap, ntiles=NT):
    sb = lambda name, shape, dt=F32: es.enter_context(nc.sbuf_tensor(name, shape, dt))
    ps = lambda name, shape, dt=F32: es.enter_context(nc.psum_tensor(name, shape, dt))
    idx_f32 = sb("idx_f32", [128, 128])
    ident = sb("ident", [128, 128], BF16)
    build_ident(sch, idx_f32, ident)
    consts = Consts(sch, nc, es, [0.0, 1e-5, 1.0])
    onesb = sb("onesb", [1, 128], BF16)
    sch.op("pool", lambda e: e.memset(onesb[:], 1.0), writes=["onesb"])

    w_sb = sb("w_sb", [128, 8, 4096], BF16)
    sch.op("dpool", lambda e: e.dma_start(
        out=w_sb[:, :, 0:2048], in_=win_ap.rearrange("(c p) n -> p c n", p=128)),
        writes=["w_sb"])
    sch.op("dpool", lambda e: e.dma_start(
        out=w_sb[:, :, 2048:4096],
        in_=wgate_ap.rearrange("(c p) n -> p c n", p=128)), writes=["w_sb"])
    b_sb = sb("b_sb", [1, 4096], BF16)
    sch.op("dpool", lambda e: e.dma_start(out=b_sb[:, 0:2048],
                                          in_=bin_ap.unsqueeze(0)),
           writes=["b_sb"])
    sch.op("dpool", lambda e: e.dma_start(out=b_sb[:, 2048:4096],
                                          in_=bgate_ap.unsqueeze(0)),
           writes=["b_sb"])

    NSL = 2
    xt = [sb(f"xt{i}", [128, 1024]) for i in range(NSL)]
    zt = [sb(f"zt{i}", [128, 1024]) for i in range(NSL)]
    stt_ = [sb(f"st{i}", [128, 8]) for i in range(NSL)]
    junk = sb("junk", [128, 2048])
    xn = [sb(f"xn{i}", [128, 1024], BF16) for i in range(NSL)]
    pbf = [ps(f"pbf{i}", [128, 1024], BF16) for i in range(2)]
    pbfn = ["pbf0", "pbf1"]
    xnf = [sb(f"xnf{i}", [128, 1024], BF16) for i in range(NSL)]
    pm = [ps(f"pm{i}", [128, 512]) for i in range(2)]
    xpo = [sb(f"xpo{i}", [128, 2048]) for i in range(NSL)]
    gto = [sb(f"gto{i}", [128, 2048]) for i in range(NSL)]

    for t in range(ntiles):
        s = t % NSL
        T = lambda n: f"{n}{s}"
        sch.op("dsp", lambda e, t=t, s=s: e.dma_start(
            out=xt[s][:], in_=xin_ap[t * 128:(t + 1) * 128, :]),
            writes=[T("xt")])
        sch.op("dsp", lambda e, t=t, s=s: e.dma_start(
            out=zt[s][:], in_=z_ap[t * 128:(t + 1) * 128, :]),
            writes=[T("zt")])
        sch.op("dve", lambda e, s=s: e.tensor_tensor(
            out=xt[s][:], in0=xt[s][:], in1=zt[s][:], op=ALU.add),
            reads=[T("zt")], writes=[T("xt")])
        sch.op("dsp", lambda e, t=t, s=s: e.dma_start(
            out=x3_ap[t * 128:(t + 1) * 128, :], in_=xt[s][:]),
            reads=[T("xt")], writes=[T("xt") + "_st"])
        ln_stats(sch, xt[s][:], T("xt"), stt_[s], T("st"), junk[:, 0:1024],
                 "junk", consts)
        ln_apply(sch, xt[s][:], T("xt"), stt_[s], T("st"), xn[s][:], T("xn"))
        transpose_plane(sch, xn[s][:], T("xn"), pbf[s], pbfn[s], xnf[s][:],
                        T("xnf"), ident, 8)
        for j in range(8):
            pp = j % 2
            mm_chain(sch, pm[pp][:], f"pm{pp}", xnf[s][:], T("xnf"),
                     lambda c, j=j: w_sb[:, c, j * 512:(j + 1) * 512],
                     "w_sb", 8,
                     extra=[(onesb[:], b_sb[:, j * 512:(j + 1) * 512])],
                     extra_reads=["onesb", "b_sb"])
            if j < 4:
                sch.op("dve", lambda e, j=j, pp=pp, s=s: e.tensor_copy(
                    xpo[s][:, j * 512:(j + 1) * 512], pm[pp][:]),
                    reads=[f"pm{pp}"], writes=[T("xpo")])
            else:
                sch.op("act", lambda e, j=j, pp=pp, s=s: e.activation(
                    out=gto[s][:, (j - 4) * 512:(j - 3) * 512], in_=pm[pp][:],
                    func=AF.Exp, scale=-1.0, bias=consts.col(0.0)),
                    reads=[f"pm{pp}", "consts"], writes=[T("gto")])
        for j in range(4):
            sch.op("dve", lambda e, j=j, s=s: e.tensor_scalar(
                out=junk[:, j * 512:(j + 1) * 512],
                in0=gto[s][:, j * 512:(j + 1) * 512], scalar1=1.0,
                scalar2=None, op0=ALU.add),
                reads=[T("gto")], writes=["junk"])
            sch.op("dve", lambda e, j=j, s=s: e.reciprocal(
                out=gto[s][:, j * 512:(j + 1) * 512],
                in_=junk[:, j * 512:(j + 1) * 512]),
                reads=["junk"], writes=[T("gto")])
        sch.op("dsp", lambda e, t=t, s=s: e.dma_start(
            out=xp_ap[t * 128:(t + 1) * 128, :], in_=xpo[s][:]),
            reads=[T("xpo")], writes=[T("xpo") + "_st"])
        sch.op("dsp", lambda e, t=t, s=s: e.dma_start(
            out=gt_ap[t * 128:(t + 1) * 128, :], in_=gto[s][:]),
            reads=[T("gto")], writes=[T("gto") + "_st"])


SSM_D = 32


def build_ssm_scan(sch, nc, es, xcs_ap, A_ap, B_ap, C_ap, ycs_ap,
                   nrows=512, Ltime=L, nch=256):
    sb = lambda name, shape, dt=F32: es.enter_context(nc.sbuf_tensor(name, shape, dt))
    nrc = nrows // 128
    ncc = nch // 128
    HALO = 47
    SEG = min(2048, Ltime)
    nseg = Ltime // SEG

    Atab = sb("Atab", [128, ncc, 16, 16])
    Btab = sb("Btab", [128, ncc, 16])
    Ctab = sb("Ctab", [128, ncc, 16])
    sch.op("dsp", lambda e: e.dma_start(
        out=Atab[:], in_=A_ap.rearrange("(cc p) n m -> p cc n m", p=128)),
        writes=["Atab"])
    sch.op("dsp", lambda e: e.dma_start(
        out=Btab[:], in_=B_ap.rearrange("(cc p) n -> p cc n", p=128)),
        writes=["Btab"])
    sch.op("dsp", lambda e: e.dma_start(
        out=Ctab[:], in_=C_ap.rearrange("(cc p) n -> p cc n", p=128)),
        writes=["Ctab"])
    vcur = sb("vcur", [128, ncc, 16])
    vnxt = sb("vnxt", [128, ncc, 16])
    prod = sb("prod", [128, 256])
    ktab = sb("ktab", [128, ncc, SSM_D])
    for cc in range(ncc):
        sch.op("dve", lambda e, cc=cc: e.tensor_copy(vcur[:, cc], Btab[:, cc]),
               reads=["Btab"], writes=["vcur"])
    for d in range(SSM_D):
        for cc in range(ncc):
            sch.op("dve", lambda e, cc=cc, d=d: e.tensor_tensor_reduce(
                out=prod[:, 0:16], in0=Ctab[:, cc], in1=vcur[:, cc],
                op=ALU.mult, accum_out=ktab[:, cc, d:d + 1])
                if hasattr(e, "no") else e.tensor_tensor(
                out=prod[:, 0:16], in0=Ctab[:, cc], in1=vcur[:, cc],
                op=ALU.mult), reads=["Ctab", "vcur"], writes=["prod"])
            sch.op("dve", lambda e, cc=cc, d=d: e.tensor_reduce(
                out=ktab[:, cc, d:d + 1], in_=prod[:, 0:16], axis=AX.X,
                op=ALU.add), reads=["prod"], writes=["ktab"])
            if d < SSM_D - 1:
                sch.op("dve", lambda e, cc=cc: e.tensor_tensor(
                    out=prod[:].rearrange("p (n m) -> p n m", n=16),
                    in0=Atab[:, cc],
                    in1=vcur[:, cc].unsqueeze(1).broadcast_to([128, 16, 16]),
                    op=ALU.mult), reads=["Atab", "vcur"], writes=["prod"])
                sch.op("dve", lambda e, cc=cc: e.tensor_reduce(
                    out=vnxt[:, cc], in_=prod[:].rearrange("p (n m) -> p n m", n=16),
                    axis=AX.X, op=ALU.add), reads=["prod"], writes=["vnxt"])
                sch.op("dve", lambda e, cc=cc: e.tensor_copy(
                    vcur[:, cc], vnxt[:, cc]), reads=["vnxt"], writes=["vcur"])

    NSL = 2
    xs = [sb(f"xs{i}", [128, HALO + SEG]) for i in range(NSL)]
    ya = [sb(f"ya{i}", [128, SEG]) for i in range(NSL)]
    yb = [sb(f"yb{i}", [128, SEG]) for i in range(NSL)]
    half = 20
    pp_ = [sb(f"pp{i}", [128, SEG]) for i in range(NSL)]
    for rc in range(nrc):
        cc = rc % ncc
        for seg in range(nseg):
            s = (rc * nseg + seg) % NSL
            sch.op("dsp", lambda e, rc=rc, seg=seg, s=s: e.dma_start(
                out=xs[s][:],
                in_=xcs_ap[rc * 128:(rc + 1) * 128, seg * SEG:seg * SEG + HALO + SEG]),
                writes=[f"xs{s}"])
            for d in range(half):
                if d == 0:
                    sch.op("dve", lambda e, s=s, cc=cc, d=d: e.tensor_scalar(
                        out=ya[s][:], in0=xs[s][:, HALO - d:HALO - d + SEG],
                        scalar1=ktab[:, cc, d:d + 1], scalar2=None,
                        op0=ALU.mult),
                        reads=[f"xs{s}", "ktab"], writes=[f"ya{s}"])
                else:
                    sch.op("dve", lambda e, s=s, cc=cc, d=d: e.scalar_tensor_tensor(
                        out=ya[s][:], in0=xs[s][:, HALO - d:HALO - d + SEG],
                        scalar=ktab[:, cc, d:d + 1], in1=ya[s][:],
                        op0=ALU.mult, op1=ALU.add),
                        reads=[f"xs{s}", "ktab", f"ya{s}"], writes=[f"ya{s}"])
            for d in range(half, SSM_D):
                sch.op("pool", lambda e, s=s, cc=cc, d=d: e.tensor_scalar(
                    out=pp_[s][:], in0=xs[s][:, HALO - d:HALO - d + SEG],
                    scalar1=ktab[:, cc, d:d + 1], scalar2=None, op0=ALU.mult),
                    reads=[f"xs{s}", "ktab"], writes=[f"pp{s}"])
                if d == half:
                    sch.op("pool", lambda e, s=s: e.tensor_copy(
                        yb[s][:], pp_[s][:]),
                        reads=[f"pp{s}"], writes=[f"yb{s}"])
                else:
                    sch.op("pool", lambda e, s=s: e.tensor_tensor(
                        out=yb[s][:], in0=yb[s][:], in1=pp_[s][:], op=ALU.add),
                        reads=[f"yb{s}", f"pp{s}"], writes=[f"yb{s}"])
            sch.op("dve", lambda e, s=s: e.tensor_tensor(
                out=ya[s][:], in0=ya[s][:], in1=yb[s][:], op=ALU.add),
                reads=[f"ya{s}", f"yb{s}"], writes=[f"ya{s}"])
            sch.op("dsp", lambda e, rc=rc, seg=seg, s=s: e.dma_start(
                out=ycs_ap[rc * 128:(rc + 1) * 128, seg * SEG:(seg + 1) * SEG],
                in_=ya[s][:]), reads=[f"ya{s}"], writes=[f"ya{s}_st"])


def build_ssm_out(sch, nc, es, xin_ap, y_ap, gt_ap, wout_ap, bout_ap, xout_ap,
                  ntiles=NT):
    sb = lambda name, shape, dt=F32: es.enter_context(nc.sbuf_tensor(name, shape, dt))
    ps = lambda name, shape, dt=F32: es.enter_context(nc.psum_tensor(name, shape, dt))
    idx_f32 = sb("idx_f32", [128, 128])
    ident = sb("ident", [128, 128], BF16)
    build_ident(sch, idx_f32, ident)
    onesb = sb("onesb", [1, 128], BF16)
    sch.op("pool", lambda e: e.memset(onesb[:], 1.0), writes=["onesb"])
    w_sb = sb("w_sb", [128, 16, 1024], BF16)
    sch.op("dpool", lambda e: e.dma_start(
        out=w_sb[:], in_=wout_ap.rearrange("(c p) n -> p c n", p=128)),
        writes=["w_sb"])
    b_sb = sb("b_sb", [1, 1024], BF16)
    sch.op("dpool", lambda e: e.dma_start(out=b_sb[:],
                                          in_=bout_ap.unsqueeze(0)),
           writes=["b_sb"])
    NSL = 2
    xt = [sb(f"xt{i}", [128, 1024]) for i in range(NSL)]
    yt = [sb(f"yt{i}", [128, 2048]) for i in range(NSL)]
    gt = [sb(f"gt{i}", [128, 2048]) for i in range(NSL)]
    u = [sb(f"u{i}", [128, 2048], BF16) for i in range(NSL)]
    pbf = [ps(f"pbf{i}", [128, 1024], BF16) for i in range(2)]
    pbfn = ["pbf0", "pbf1"]
    uf = [sb(f"uf{i}", [128, 2048], BF16) for i in range(NSL)]
    pm = [ps(f"pm{i}", [128, 512]) for i in range(2)]
    xo = [sb(f"xo{i}", [128, 1024]) for i in range(NSL)]
    for t in range(ntiles):
        s = t % NSL
        T = lambda n: f"{n}{s}"
        for (ap, tile, nm) in ((xin_ap, xt, "xt"), (y_ap, yt, "yt"),
                               (gt_ap, gt, "gt")):
            sch.op("dsp", lambda e, ap=ap, tile=tile, t=t, s=s: e.dma_start(
                out=tile[s][:], in_=ap[t * 128:(t + 1) * 128, :]),
                writes=[T(nm)])
        sch.op("dve", lambda e, s=s: e.tensor_tensor(
            out=u[s][:], in0=yt[s][:], in1=gt[s][:], op=ALU.mult),
            reads=[T("yt"), T("gt")], writes=[T("u")])
        for g in range(2):
            transpose_plane(sch, u[s][:, g * 1024:(g + 1) * 1024], T("u"),
                            pbf[g], pbfn[g],
                            uf[s][:, g * 1024:(g + 1) * 1024], T("uf"),
                            ident, 8)
        for oc in range(2):
            mm_chain(sch, pm[oc][:], f"pm{oc}", uf[s][:], T("uf"),
                     lambda c, oc=oc: w_sb[:, c, oc * 512:(oc + 1) * 512],
                     "w_sb", 16,
                     extra=[(onesb[:], b_sb[:, oc * 512:(oc + 1) * 512])],
                     extra_reads=["onesb", "b_sb"])
            sch.op("dve", lambda e, oc=oc, s=s: e.scalar_tensor_tensor(
                out=xo[s][:, oc * 512:(oc + 1) * 512], in0=pm[oc][:],
                scalar=1.0, in1=xt[s][:, oc * 512:(oc + 1) * 512],
                op0=ALU.mult, op1=ALU.add),
                reads=[f"pm{oc}", T("xt")], writes=[T("xo")])
        sch.op("dsp", lambda e, t=t, s=s: e.dma_start(
            out=xout_ap[t * 128:(t + 1) * 128, :], in_=xo[s][:]),
            reads=[T("xo")], writes=[T("xo") + "_st"])


HALO_T = 6


def barrier(sch):
    sch._barrier = [s.ops[-1] for s in sch.streams.values() if s.ops]


def build_l0ffn_depth(sch, nc, es, xh_ap, wqkv_ap, qs_ap, ks_ap, wo_ap,
                      lng_ap, lnb_ap, w1_ap, b1_ap, w2_ap, b2_ap, kb_ap,
                      pos0_ap, xout_ap, nq=NT, pre_ln=False):
    sb = lambda name, shape, dt=F32: es.enter_context(nc.sbuf_tensor(name, shape, dt))
    ps = lambda name, shape, dt=F32: es.enter_context(nc.psum_tensor(name, shape, dt))
    ngt = nq + HALO_T

    xprime = nc.dram_tensor("xprime", [(nq + 1) * 128, D], F32,
                            kind="Internal").ap()

    idx_f32 = sb("idx_f32", [128, 128])
    ident = sb("ident", [128, 128], BF16)
    build_ident(sch, idx_f32, ident)
    consts = Consts(sch, nc, es, [0.0, 1e-5, 1e-24, 1.0, 25.6 / 89.6, 1e-38])
    onesb = sb("onesb", [1, 128], BF16)
    sch.op("pool", lambda e: e.memset(onesb[:], 1.0), writes=["onesb"])
    ones_f = sb("ones_f", [1, 128], F32)
    sch.op("pool", lambda e: e.memset(ones_f[:], 1.0), writes=["ones_f"])
    rowi = sb("rowi", [128, 1], F32)
    sch.op("pool", lambda e: e.iota(rowi[:], pattern=[[0, 1]], base=0,
                                    channel_multiplier=1,
                                    allow_small_or_imprecise_dtypes=True),
           writes=["rowi"])
    maskc = sb("maskc", [128, 256], BF16)
    sch.op("dve", lambda e: e.tensor_scalar(out=maskc[:, 0:128], in0=idx_f32[:],
                                            scalar1=0.0, scalar2=None,
                                            op0=ALU.is_le),
           reads=["idx_f32"], writes=["maskc"])
    sch.op("dve", lambda e: e.tensor_scalar(out=maskc[:, 128:256], in0=idx_f32[:],
                                            scalar1=0.0, scalar2=None,
                                            op0=ALU.is_ge),
           reads=["idx_f32"], writes=["maskc"])
    sel = sb("sel", [128, 2], F32)
    sch.op("dve", lambda e: e.tensor_scalar(out=sel[:, 0:1], in0=rowi[:],
                                            scalar1=64.0, scalar2=None,
                                            op0=ALU.is_lt),
           reads=["rowi"], writes=["sel"])
    sch.op("dve", lambda e: e.tensor_scalar(out=sel[:, 1:2], in0=rowi[:],
                                            scalar1=64.0, scalar2=None,
                                            op0=ALU.is_ge),
           reads=["rowi"], writes=["sel"])

    pos0_sb = sb("pos0_sb", [1, 1], F32)
    sch.op("dsp", lambda e: e.dma_start(out=pos0_sb[:], in_=pos0_ap),
           writes=["pos0_sb"])
    pmisc = ps("pmisc", [128, 128], F32)
    sch.op("pe", lambda e: e.matmul(pmisc[:, 0:1], ones_f[:], pos0_sb[:],
                                    start=True, stop=True),
           reads=["ones_f", "pos0_sb"], writes=["pmisc"])
    posb = sb("posb", [128, 1], F32)
    sch.op("dve", lambda e: e.scalar_tensor_tensor(
        out=posb[:], in0=pmisc[:, 0:1], scalar=1.0, in1=rowi[:],
        op0=ALU.mult, op1=ALU.add), reads=["pmisc", "rowi"], writes=["posb"])
    lb = sb("lb", [128, 32], F32)
    sch.op("pool", lambda e: e.iota(lb[:], pattern=[[1, 32]], base=0,
                                    channel_multiplier=0,
                                    allow_small_or_imprecise_dtypes=True),
           writes=["lb"])
    sch.op("act", lambda e: e.activation(out=lb[:], in_=lb[:], func=AF.Ln,
                                         scale=2.0 / 89.6,
                                         bias=consts.col(25.6 / 89.6)),
           reads=["lb", "consts"], writes=["lb"])
    invfreq = sb("invfreq", [128, 32], F32)
    sch.op("pool", lambda e: e.iota(invfreq[:], pattern=[[2, 32]], base=0,
                                    channel_multiplier=0,
                                    allow_small_or_imprecise_dtypes=True),
           writes=["invfreq"])
    sch.op("act", lambda e: e.activation(out=invfreq[:], in_=invfreq[:],
                                         func=AF.Exp,
                                         scale=-float(np.log(10000.0)) / 64,
                                         bias=consts.col(0.0)),
           reads=["invfreq", "consts"], writes=["invfreq"])
    esT = ExitStack()
    sbT = lambda name, shape, dt=F32: esT.enter_context(
        nc.sbuf_tensor(name, shape, dt))
    tabs = sbT("tabs", [128, ngt, 2, 32], F32)
    rope_tables(sch, nc, esT, consts, posb, tabs, ngt, invfreq)
    sctab = sbT("sctab", [128, ngt, 2, 32], F32)
    pw = sbT("pw", [128, 1], F32)
    for t in range(ngt):
        sch.op("dve", lambda e, t=t: e.tensor_scalar(
            out=pw[:], in0=posb[:], scalar1=1.0 / 128.0,
            scalar2=float(128 * t - 4096) / 128.0, op0=ALU.mult, op1=ALU.add)
            if t >= 0 else None,
            reads=["posb"], writes=["pw"])
        sch.op("dve", lambda e, t=t: e.tensor_scalar(
            out=sctab[:, t, 0], in0=lb[:], scalar1=pw[:], scalar2=None,
            op0=ALU.mult), reads=["lb", "pw"], writes=["sctab"])
        sch.op("act", lambda e, t=t: e.activation(
            out=sctab[:, t, 1], in_=sctab[:, t, 0], func=AF.Exp, scale=-1.0,
            bias=consts.col(0.0)), reads=["sctab", "consts"], writes=["sctab"])
        sch.op("act", lambda e, t=t: e.activation(
            out=sctab[:, t, 0], in_=sctab[:, t, 0], func=AF.Exp, scale=1.0,
            bias=consts.col(0.0)), reads=["sctab", "consts"], writes=["sctab"])
    ABq = sbT("ABq", [128, ngt, 2, 64], BF16)
    ABk = sbT("ABk", [128, ngt, 2, 64], BF16)
    tmp32 = sbT("tmp32", [128, 32], F32)
    for t in range(ngt):
        for (dst, dbuf, sci) in ((ABq, "ABq", 0), (ABk, "ABk", 1)):
            sch.op("dve", lambda e, t=t, dst=dst, sci=sci: e.tensor_tensor(
                out=dst[:, t, 0, 0:32], in0=tabs[:, t, 0],
                in1=sctab[:, t, sci], op=ALU.mult),
                reads=["tabs", "sctab"], writes=[dbuf])
            sch.op("dve", lambda e, t=t, dst=dst: e.tensor_copy(
                dst[:, t, 0, 32:64], dst[:, t, 0, 0:32]),
                reads=[dbuf], writes=[dbuf])
            sch.op("dve", lambda e, t=t, sci=sci: e.tensor_tensor(
                out=tmp32[:], in0=tabs[:, t, 1], in1=sctab[:, t, sci],
                op=ALU.mult), reads=["tabs", "sctab"], writes=["tmp32"])
            sch.op("dve", lambda e, t=t, dst=dst: e.tensor_scalar(
                out=dst[:, t, 1, 0:32], in0=tmp32[:], scalar1=-1.0,
                scalar2=None, op0=ALU.mult), reads=["tmp32"], writes=[dbuf])
            sch.op("dve", lambda e, t=t, dst=dst: e.tensor_copy(
                dst[:, t, 1, 32:64], tmp32[:]), reads=["tmp32"], writes=[dbuf])

    kb_sb = sbT("kb_sb", [128, ngt], F32)
    sch.op("dsp", lambda e: e.dma_start(out=kb_sb[:], in_=kb_ap),
           writes=["kb_sb"])

    esA = ExitStack()
    with esA:
        sbA = lambda name, shape, dt=F32: esA.enter_context(
            nc.sbuf_tensor(name, shape, dt))
        wqkv_sb = sbA("wqkv_sb", [128, 8, 3072], BF16)
        sch.op("dpool", lambda e: e.dma_start(
            out=wqkv_sb[:], in_=wqkv_ap.rearrange("(c p) n -> p c n", p=128)),
            writes=["wqkv_sb"])
        qsrow = sbA("qsrow", [1, 128], F32)
        sch.op("dsp", lambda e: e.dma_start(out=qsrow[:, 0:64], in_=qs_ap.unsqueeze(0)),
               writes=["qsrow"])
        sch.op("dsp", lambda e: e.dma_start(out=qsrow[:, 64:128], in_=ks_ap.unsqueeze(0)),
               writes=["qsrow"])
        qsrep = sbA("qsrep", [128, 128], F32)
        sch.op("pe", lambda e: e.matmul(pmisc[:], ones_f[:], qsrow[:],
                                        start=True, stop=True),
               reads=["ones_f", "qsrow"], writes=["pmisc"])
        sch.op("dve", lambda e: e.tensor_copy(qsrep[:], pmisc[:]),
               reads=["pmisc"], writes=["qsrep"])
        for h in range(16):
            sch.op("pool", lambda e, h=h: e.tensor_tensor(
                out=wqkv_sb[:, :, h * 64:(h + 1) * 64],
                in0=wqkv_sb[:, :, h * 64:(h + 1) * 64],
                in1=qsrep[:, 0:64].unsqueeze(1).broadcast_to([128, 8, 64]),
                op=ALU.mult), reads=["wqkv_sb", "qsrep"], writes=["wqkv_sb"])
            sch.op("pool", lambda e, h=h: e.tensor_tensor(
                out=wqkv_sb[:, :, 1024 + h * 64:1024 + (h + 1) * 64],
                in0=wqkv_sb[:, :, 1024 + h * 64:1024 + (h + 1) * 64],
                in1=qsrep[:, 64:128].unsqueeze(1).broadcast_to([128, 8, 64]),
                op=ALU.mult), reads=["wqkv_sb", "qsrep"], writes=["wqkv_sb"])
        wo_sb = sbA("wo_sb", [128, 8, 1024], BF16)
        sch.op("dpool", lambda e: e.dma_start(
            out=wo_sb[:], in_=wo_ap.rearrange("(c p) n -> p c n", p=128)),
            writes=["wo_sb"])

        NSL = 2
        KR = 4
        psA = lambda name, shape, dt=F32: esA.enter_context(
            nc.psum_tensor(name, shape, dt))
        xt = [sbA(f"xt{i}", [128, 1024]) for i in range(NSL)]
        stt_ = [sbA(f"st{i}", [128, 8]) for i in range(NSL)]
        junk = sbA("junk", [128, 1024])
        xbf = [sbA(f"xbf{i}", [128, 1024], BF16) for i in range(NSL)]
        pbf = [psA(f"pbf{i}", [128, 1024], BF16) for i in range(2)]
        pbfn = ["pbf0", "pbf1"]
        xf = [sbA(f"xf{i}", [128, 8, 129], BF16) for i in range(NSL)]
        pqk = psA("pqk", [128, 512])
        qtm = sbA("qtm", [128, 1024], BF16)
        ktm = sbA("ktm", [128, 1024], BF16)
        rinv = sbA("rinv", [128, 16])
        ss16 = sbA("ss16", [128, 16])
        G = sbA("G", [128, 1024], BF16)
        P1 = sbA("P1", [128, 1024], BF16)
        qbf = sbA("qbf", [128, 1024], BF16)
        kbf = sbA("kbf", [128, 1024], BF16)
        kfm = sbA("kfm", [66, KR, 16, 128], BF16)
        qfm = [sbA(f"qfm{i}", [66, 16, 128], BF16) for i in range(NSL)]
        absq = sbA("absq", [64, 16, 128], BF16)
        absbf = sbA("absbf", [128, 1024], BF16)
        for i in range(NSL):
            sch.op("pool", lambda e, i=i: e.memset(qfm[i][64:66, :, :], 1.0),
                   writes=[f"qfm{i}"])
        kb2col = sbA("kb2col", [128, 2], F32)
        sch.op("pool", lambda e: e.memset(kb2col[:, 0:1], 1.0), writes=["kb2col"])
        kbrow2 = sbA("kbrow2", [2, 128], BF16)
        vtm = sbA("vtm", [128, KR, 16, 66], BF16)
        sch.op("pool", lambda e: e.memset(vtm[:], 1.0), writes=["vtm"])
        Kbar = sbA("Kbar", [128, KR, 8])
        Kb2 = sbA("Kb2", [64, 16], BF16)
        m1tm = sbA("m1tm", [128, 16])
        m2tm = sbA("m2tm", [128, 1])
        onescol = sbA("onescol", [128, 1], BF16)
        sch.op("pool", lambda e: e.memset(onescol[:], 1.0), writes=["onescol"])
        pS = psA("pS", [128, 4, 384])
        pD = psA("pD", [128, 16])
        U = sbA("U", [128, 4, 384], BF16)
        U2 = sbA("U2", [128, 4, 384], BF16)
        pO = pqk.rearrange("p (h d) -> p h d", h=4) if False else None
        Rr = sbA("Rr", [128, 4])
        otm = sbA("otm", [128, 1024], BF16)
        ofm = sbA("ofm", [128, 1024], BF16)
        xo = sbA("xo", [128, 1024])

        def qkv_tile(gt):
            s = gt % NSL
            slot = gt % KR
            T = lambda n: f"{n}{s}"
            own = gt >= HALO_T - 1
            sch.op("dsp", lambda e, gt=gt, s=s: e.dma_start(
                out=xt[s][:], in_=xh_ap[gt * 128:(gt + 1) * 128, :]),
                writes=[T("xt")])
            if pre_ln:
                ln_stats(sch, xt[s][:], T("xt"), stt_[s], T("st"), junk[:],
                         "junk", consts)
                sch.op("dve", lambda e, s=s: e.tensor_scalar(
                    out=xt[s][:], in0=xt[s][:], scalar1=stt_[s][:, 2:3],
                    scalar2=stt_[s][:, 3:4], op0=ALU.mult, op1=ALU.add),
                    reads=[T("st")], writes=[T("xt")])
            sch.op("pool", lambda e, s=s: e.tensor_copy(xbf[s][:], xt[s][:]),
                   reads=[T("xt")], writes=[T("xbf")])
            for c in range(8):
                sch.op("pe", lambda e, c=c, s=s: e.transpose(
                    pbf[s][:, c * 128:(c + 1) * 128],
                    xbf[s][:, c * 128:(c + 1) * 128], ident[:]),
                    reads=[T("xbf"), "ident"], writes=[pbfn[s]])
            sp = (gt - 1) % NSL
            if gt > 0:
                sch.op("dve", lambda e, s=s, sp=sp: e.tensor_copy(
                    xf[s][:, :, 0:1], xf[sp][:, :, 128:129]),
                    reads=[f"xf{sp}"], writes=[T("xf")])
            else:
                sch.op("pool", lambda e, s=s: e.memset(xf[s][:, :, 0:1], 0.0),
                       writes=[T("xf")])
            sch.op("dve", lambda e, s=s: e.tensor_copy(
                xf[s][:, :, 1:129],
                pbf[s][:].rearrange("p (c t) -> p c t", c=8)),
                reads=[pbfn[s]], writes=[T("xf")])
            for j in range(6):
                if j < 2 and not own:
                    continue
                def lhs(c, s=s):
                    return (xf[s][:, c, 1:129] if c < 4
                            else xf[s][:, c, 0:128])
                for c in range(8):
                    sch.op("pe", lambda e, c=c, j=j, s=s: e.matmul(
                        pqk[:], lhs(c),
                        wqkv_sb[:, c, j * 512:(j + 1) * 512],
                        start=(c == 0), stop=(c == 7)),
                        reads=[T("xf"), "wqkv_sb"], writes=["pqk"])
                if j < 4:
                    dst, dbuf = (qtm, "qtm") if j < 2 else (ktm, "ktm")
                    jj = j % 2
                    sch.op("dve", lambda e, dst=dst, jj=jj: e.tensor_copy(
                        dst[:, jj * 512:(jj + 1) * 512], pqk[:]),
                        reads=["pqk"], writes=[dbuf])
                else:
                    h0 = (j - 4) * 8
                    sch.op("dve", lambda e, h0=h0, slot=slot: e.tensor_copy(
                        vtm[:, slot, h0:h0 + 8, 0:64],
                        pqk[:].rearrange("p (h d) -> p h d", h=8)),
                        reads=["pqk"], writes=["vtm"])
            for (src, sbuf_, dst, dbuf, ab, abn, do) in (
                    (qtm, "qtm", qbf, "qbf", ABq, "ABq", own),
                    (ktm, "ktm", kbf, "kbf", ABk, "ABk", True)):
                if not do:
                    continue
                sch.op("pool", lambda e, src=src: e.tensor_tensor(
                    out=junk[:], in0=src[:], in1=src[:], op=ALU.mult),
                    reads=[sbuf_], writes=["junk"])
                sch.op("dve", lambda e: e.tensor_reduce(
                    out=ss16[:], in_=junk[:].rearrange("p (h d) -> p h d", h=16),
                    axis=AX.X, op=ALU.add), reads=["junk"], writes=["ss16"])
                sch.op("act", lambda e: e.activation(
                    out=rinv[:], in_=ss16[:], func=AF.Ln,
                    bias=consts.col(1e-24), scale=1.0),
                    reads=["ss16", "consts"], writes=["rinv"])
                sch.op("act", lambda e: e.activation(
                    out=rinv[:], in_=rinv[:], func=AF.Exp, scale=-0.5,
                    bias=consts.col(0.0)),
                    reads=["rinv", "consts"], writes=["rinv"])
                sch.op("pool", lambda e, src=src: e.tensor_tensor(
                    out=G[:].rearrange("p (h d) -> p h d", h=16),
                    in0=src[:].rearrange("p (h d) -> p h d", h=16),
                    in1=rinv[:].unsqueeze(2).broadcast_to([128, 16, 64]),
                    op=ALU.mult), reads=[sbuf_, "rinv"], writes=["G"])
                sch.op("dve", lambda e, ab=ab, gt=gt: e.tensor_tensor(
                    out=P1[:].rearrange("p (h d) -> p h d", h=16),
                    in0=G[:].rearrange("p (h d) -> p h d", h=16),
                    in1=ab[:, gt, 0, :].unsqueeze(1).broadcast_to([128, 16, 64]),
                    op=ALU.mult), reads=["G", abn], writes=["P1"])
                swapG = bass.AP(G[:].tensor, 32,
                                [G[:].ap[0], [64, 16], [-32, 2], [1, 32]])
                sch.op("dve", lambda e, ab=ab, gt=gt, swapG=swapG: e.tensor_tensor(
                    out=junk[:, 0:1024].rearrange("p (h a d) -> p h a d", h=16, a=2),
                    in0=swapG,
                    in1=ab[:, gt, 1, :].unsqueeze(1).broadcast_to([128, 16, 64])
                        .rearrange("p h (a d) -> p h a d", a=2),
                    op=ALU.mult), reads=["G", abn], writes=["junk"])
                sch.op("dve", lambda e, dst=dst: e.tensor_tensor(
                    out=dst[:], in0=P1[:], in1=junk[:, 0:1024], op=ALU.add),
                    reads=["P1", "junk"], writes=[dbuf])
            sch.op("dve", lambda e, gt=gt: e.tensor_copy(
                kb2col[:, 1:2], kb_sb[:, gt:gt + 1]),
                reads=["kb_sb"], writes=["kb2col"])
            sch.op("pe", lambda e: e.transpose(
                pmisc[0:2, 0:128], kb2col[:], identf8[:]),
                reads=["kb2col", "identf8"], writes=["pmisc"])
            sch.op("dve", lambda e: e.tensor_copy(kbrow2[:], pmisc[0:2, 0:128]),
                   reads=["pmisc"], writes=["kbrow2"])
            sch.op("dve", lambda e, slot=slot: e.tensor_copy(
                kfm[64:66, slot, :, :],
                kbrow2[:].unsqueeze(1).broadcast_to([2, 16, 128])),
                reads=["kbrow2"], writes=["kfm"])
            if own:
                for c in range(8):
                    sch.op("pe", lambda e, c=c, s=s: e.transpose(
                        pbf[s][:, c * 128:(c + 1) * 128],
                        qbf[:, c * 128:(c + 1) * 128], ident[:]),
                        reads=["qbf", "ident"], writes=[pbfn[s]])
                sch.op("dve", lambda e, s=s: e.tensor_copy(
                    qfm[s][0:64, 0:16:2, :],
                    pbf[s][0:64, :].rearrange("p (c t) -> p c t", c=8)),
                    reads=[pbfn[s]], writes=[T("qfm")])
                sch.op("dve", lambda e, s=s: e.tensor_copy(
                    qfm[s][0:64, 1:16:2, :],
                    pbf[s][64:128, :].rearrange("p (c t) -> p c t", c=8)),
                    reads=[pbfn[s]], writes=[T("qfm")])
                sch.op("act", lambda e: e.activation(
                    out=absbf[:], in_=qbf[:], func=AF.Abs,
                    bias=consts.col(0.0), scale=1.0),
                    reads=["qbf", "consts"], writes=["absbf"])
                for c in range(8):
                    sch.op("pe", lambda e, c=c, s=s: e.transpose(
                        pbf[s][:, c * 128:(c + 1) * 128],
                        absbf[:, c * 128:(c + 1) * 128], ident[:]),
                        reads=["absbf", "ident"], writes=[pbfn[s]])
                sch.op("dve", lambda e, s=s: e.tensor_copy(
                    absq[:, 0:16:2, :],
                    pbf[s][0:64, :].rearrange("p (c t) -> p c t", c=8)),
                    reads=[pbfn[s]], writes=["absq"])
                sch.op("dve", lambda e, s=s: e.tensor_copy(
                    absq[:, 1:16:2, :],
                    pbf[s][64:128, :].rearrange("p (c t) -> p c t", c=8)),
                    reads=[pbfn[s]], writes=["absq"])
            for c in range(8):
                sch.op("pe", lambda e, c=c, s=s: e.transpose(
                    pbf[s][:, c * 128:(c + 1) * 128],
                    kbf[:, c * 128:(c + 1) * 128], ident[:]),
                    reads=["kbf", "ident"], writes=[pbfn[s]])
            sch.op("dve", lambda e, s=s, slot=slot: e.tensor_reduce(
                out=Kbar[:, slot, :],
                in_=pbf[s][:].rearrange("p (c t) -> p c t", c=8),
                axis=AX.X, op=ALU.max, apply_absolute_value=True),
                reads=[pbfn[s]], writes=["Kbar"])
            sch.op("dve", lambda e, s=s, slot=slot: e.tensor_copy(
                kfm[0:64, slot, 0:16:2, :],
                pbf[s][0:64, :].rearrange("p (c t) -> p c t", c=8)),
                reads=[pbfn[s]], writes=["kfm"])
            sch.op("dve", lambda e, s=s, slot=slot: e.tensor_copy(
                kfm[0:64, slot, 1:16:2, :],
                pbf[s][64:128, :].rearrange("p (c t) -> p c t", c=8)),
                reads=[pbfn[s]], writes=["kfm"])

        def attn_tile(gt):
            s = gt % NSL
            sl = [(gt - 2) % KR, (gt - 1) % KR, gt % KR]
            sch.op("dve", lambda e, sl=sl: e.tensor_tensor(
                out=Kbar[:, sl[2], :], in0=Kbar[:, sl[0], :],
                in1=Kbar[:, sl[1], :], op=ALU.max)
                if False else e.tensor_tensor(
                out=m1tm[:, 0:8], in0=Kbar[:, sl[0], :],
                in1=Kbar[:, sl[1], :], op=ALU.max),
                reads=["Kbar"], writes=["m1tm"])
            sch.op("dve", lambda e, sl=sl: e.tensor_tensor(
                out=m1tm[:, 0:8], in0=m1tm[:, 0:8], in1=Kbar[:, sl[2], :],
                op=ALU.max), reads=["m1tm", "Kbar"], writes=["m1tm"])
            sch.op("dve", lambda e: e.tensor_copy(Kb2[:, 0:16:2],
                                                  m1tm[0:64, 0:8]),
                   reads=["m1tm"], writes=["Kb2"])
            sch.op("dve", lambda e: e.tensor_copy(Kb2[:, 1:16:2],
                                                  m1tm[64:128, 0:8]),
                   reads=["m1tm"], writes=["Kb2"])
            for h in range(16):
                sch.op("pe", lambda e, h=h, s=s: e.matmul(
                    pD[:, h:h + 1], absq[:, h, :], Kb2[:, h:h + 1],
                    start=True, stop=True),
                    reads=["absq", "Kb2"], writes=["pD"])
            sch.op("dve", lambda e: e.tensor_copy(m1tm[:], pD[:]),
                   reads=["pD"], writes=["m1tm"])
            for h in range(16):
                sch.op("pe", lambda e, h=h: e.transpose(
                    pmisc[0:1, 0:128], m1tm[:, h:h + 1], identf8[:]),
                    reads=["m1tm", "identf8"], writes=["pmisc"])
                sch.op("dve", lambda e, h=h, s=s: e.tensor_scalar(
                    out=qfm[s][64:65, h, :], in0=pmisc[0:1, 0:128],
                    scalar1=-1.0, scalar2=None, op0=ALU.mult),
                    reads=["pmisc"], writes=[f"qfm{s}"])
            for hg in range(4):
                for hh in range(4):
                    h = hg * 4 + hh
                    for i in range(3):
                        kt = gt - 2 + i
                        sch.op("pe", lambda e, i=i, hh=hh, h=h, kt=kt, s=s:
                               e.matmul(pS[:, hh, i * 128:(i + 1) * 128],
                                        kfm[:, kt % KR, h, :],
                                        qfm[s][:, h, :], start=True, stop=True),
                               reads=["kfm", f"qfm{s}"], writes=["pS"])
                sch.op("act", lambda e: e.activation(
                    out=U[:], in_=pS[:, :, 0:384], func=AF.Exp, scale=1.0,
                    bias=consts.col(0.0)),
                    reads=["pS", "consts"], writes=["U"])
                sch.op("pool", lambda e: e.tensor_tensor(
                    out=bass.AP(U[:].tensor, 0, [U[:].ap[0], [384, 4], [256, 2], [1, 128]]),
                    in0=bass.AP(U[:].tensor, 0, [U[:].ap[0], [384, 4], [256, 2], [1, 128]]),
                    in1=maskc[:].unsqueeze(1).broadcast_to([128, 4, 256]).rearrange("p a (b c) -> p a b c", b=2),
                    op=ALU.mult), reads=["U", "maskc"], writes=["U"])
                for hh in range(4):
                    for i in range(3):
                        sch.op("pe", lambda e, i=i, hh=hh: e.matmul(
                            pD[:, hh:hh + 1],
                            U[:, hh, i * 128:(i + 1) * 128], onescol[:],
                            start=(i == 0), stop=(i == 2)),
                            reads=["U", "onescol"], writes=["pD"])
                sch.op("act", lambda e, hg=hg: e.activation(
                    out=m2tm[:] if False else U2[:, 0, 0:4], in_=pD[:, 0:4],
                    func=AF.Ln, bias=consts.col(1e-38), scale=1.0),
                    reads=["pD", "consts"], writes=["U2"])
                for hh in range(4):
                    h = hg * 4 + hh
                    sch.op("dve", lambda e, h=h, hh=hh: e.scalar_tensor_tensor(
                        out=m2tm[:], in0=U2[:, 0, hh:hh + 1], scalar=1.0,
                        in1=m1tm[:, h:h + 1], op0=ALU.mult, op1=ALU.add),
                        reads=["U2", "m1tm"], writes=["m2tm"])
                    sch.op("pe", lambda e: e.transpose(
                        pmisc[0:1, 0:128], m2tm[:], identf8[:]),
                        reads=["m2tm", "identf8"], writes=["pmisc"])
                    sch.op("dve", lambda e, h=h, s=s: e.tensor_scalar(
                        out=qfm[s][64:65, h, :], in0=pmisc[0:1, 0:128],
                        scalar1=-1.0, scalar2=None, op0=ALU.mult),
                        reads=["pmisc"], writes=[f"qfm{s}"])
                for hh in range(4):
                    h = hg * 4 + hh
                    for i in range(3):
                        kt = gt - 2 + i
                        sch.op("pe", lambda e, i=i, hh=hh, h=h, kt=kt, s=s:
                               e.matmul(pS[:, hh, i * 128:(i + 1) * 128],
                                        kfm[:, kt % KR, h, :],
                                        qfm[s][:, h, :], start=True, stop=True),
                               reads=["kfm", f"qfm{s}"], writes=["pS"])
                sch.op("act", lambda e: e.activation(
                    out=U[:], in_=pS[:, :, 0:384], func=AF.Exp, scale=2.0,
                    bias=consts.col(0.0)),
                    reads=["pS", "consts"], writes=["U"])
                sch.op("pool", lambda e: e.tensor_tensor(
                    out=U2[:], in0=U[:], in1=U[:], op=ALU.mult),
                    reads=["U"], writes=["U2"])
                sch.op("pool", lambda e: e.tensor_tensor(
                    out=U[:], in0=U2[:], in1=U2[:], op=ALU.mult),
                    reads=["U2"], writes=["U"])
                sch.op("pool", lambda e: e.tensor_tensor(
                    out=bass.AP(U[:].tensor, 0, [U[:].ap[0], [384, 4], [256, 2], [1, 128]]),
                    in0=bass.AP(U[:].tensor, 0, [U[:].ap[0], [384, 4], [256, 2], [1, 128]]),
                    in1=maskc[:].unsqueeze(1).broadcast_to([128, 4, 256]).rearrange("p a (b c) -> p a b c", b=2),
                    op=ALU.mult), reads=["U", "maskc"], writes=["U"])
                for hh in range(4):
                    h = hg * 4 + hh
                    for i in range(3):
                        kt = gt - 2 + i
                        sch.op("pe", lambda e, i=i, hh=hh, h=h, kt=kt: e.matmul(
                            pqk[:, hh * 128:hh * 128 + 65],
                            U[:, hh, i * 128:(i + 1) * 128],
                            vtm[:, kt % KR, h, 0:65],
                            start=(i == 0), stop=(i == 2)),
                            reads=["U", "vtm"], writes=["pqk"])
                pOv = pqk[:].rearrange("p (h d) -> p h d", h=4)
                sch.op("dve", lambda e, pOv=pOv: e.tensor_scalar(
                    out=Rr[:], in0=pOv[:, :, 64:65].squeeze(2),
                    scalar1=1e-38, scalar2=None, op0=ALU.add),
                    reads=["pqk"], writes=["Rr"])
                sch.op("dve", lambda e: e.reciprocal(out=Rr[:], in_=Rr[:]),
                    reads=["Rr"], writes=["Rr"])
                sch.op("dve", lambda e, hg=hg, pOv=pOv: e.tensor_tensor(
                    out=otm[:].rearrange("p (h d) -> p h d", h=16)[:, hg * 4:(hg + 1) * 4, :],
                    in0=pOv[:, :, 0:64],
                    in1=Rr[:].unsqueeze(2).broadcast_to([128, 4, 64]),
                    op=ALU.mult), reads=["pqk", "Rr"], writes=["otm"])
            for c in range(8):
                sch.op("pe", lambda e, c=c, s=s: e.transpose(
                    pbf[s][:, c * 128:(c + 1) * 128],
                    otm[:, c * 128:(c + 1) * 128], ident[:]),
                    reads=["otm", "ident"], writes=[pbfn[s]])
            sch.op("dve", lambda e, s=s: e.tensor_copy(ofm[:], pbf[s][:]),
                   reads=[pbfn[s]], writes=["ofm"])
            for oc in range(2):
                mm_chain(sch, pqk[:], "pqk", ofm[:], "ofm",
                         lambda c, oc=oc: wo_sb[:, c, oc * 512:(oc + 1) * 512],
                         "wo_sb", 8)
                sch.op("dve", lambda e, oc=oc, s=s: e.scalar_tensor_tensor(
                    out=xo[:, oc * 512:(oc + 1) * 512], in0=pqk[:], scalar=1.0,
                    in1=xt[s][:, oc * 512:(oc + 1) * 512], op0=ALU.mult,
                    op1=ALU.add), reads=["pqk", f"xt{s}"], writes=["xo"])
            sch.op("dsp", lambda e, gt=gt: e.dma_start(
                out=xprime[(gt - (HALO_T - 1)) * 128:(gt - HALO_T + 2) * 128, :],
                in_=xo[:]), reads=["xo"], writes=["xo_st"])

        identf8 = sbA("identf8", [128, 128], F32)
        sch.op("dve", lambda e: e.tensor_copy(identf8[:], ident[:]),
               reads=["ident"], writes=["identf8"])

        for gt in range(ngt):
            qkv_tile(gt)
            if gt >= HALO_T - 1:
                attn_tile(gt)
        barrier(sch)
    esT.close()

    esB = ExitStack()
    with esB:
        sbB = lambda name, shape, dt=F32: esB.enter_context(
            nc.sbuf_tensor(name, shape, dt))
        w1_sb = sbB("w1_sb", [128, 8, 5460], BF16)
        sch.op("dpool", lambda e: e.dma_start(
            out=w1_sb[:], in_=w1_ap.rearrange("(c p) n -> p c n", p=128)),
            writes=["w1_sb"])
        w2_sb = sbB("w2_sb", [128, 22, 1024], BF16)
        sch.op("dpool", lambda e: e.dma_start(
            out=w2_sb[:], in_=w2_ap.rearrange("(c p) n -> p c n", p=128)),
            writes=["w2_sb"])
        lng_sb = sbB("lng_sb", [128, 8], F32)
        sch.op("dsp", lambda e: e.dma_start(out=lng_sb[:], in_=lng_ap),
               writes=["lng_sb"])
        lnb_sb = sbB("lnb_sb", [128, 8], F32)
        sch.op("dsp", lambda e: e.dma_start(out=lnb_sb[:], in_=lnb_ap),
               writes=["lnb_sb"])
        lnb_bf = sbB("lnb_bf", [128, 8], BF16)
        sch.op("dve", lambda e: e.tensor_copy(lnb_bf[:], lnb_sb[:]),
               reads=["lnb_sb"], writes=["lnb_bf"])
        b1_sb = sbB("b1_sb", [1, 5460], F32)
        sch.op("dsp", lambda e: e.dma_start(out=b1_sb[:], in_=b1_ap.unsqueeze(0)),
               writes=["b1_sb"])
        b2_sb = sbB("b2_sb", [1, 1024], BF16)
        sch.op("dpool", lambda e: e.dma_start(out=b2_sb[:], in_=b2_ap.unsqueeze(0)),
               writes=["b2_sb"])
        bb1 = sbB("bb1", [1, 5460], BF16)
        pg = [esB.enter_context(nc.psum_tensor(f"pg{i}", [128, 512], F32))
              for i in range(2)]
        for j in range(11):
            n0, n1 = j * 512, min(5460, (j + 1) * 512)
            pp = j % 2
            for c in range(8):
                sch.op("pe", lambda e, c=c, n0=n0, n1=n1, pp=pp: e.matmul(
                    pg[pp][0:1, 0:n1 - n0], lnb_bf[:, c:c + 1],
                    w1_sb[:, c, n0:n1], start=(c == 0), stop=(c == 7)),
                    reads=["lnb_bf", "w1_sb"], writes=[f"pg{pp}"])
            sch.op("dve", lambda e, n0=n0, n1=n1, pp=pp: e.scalar_tensor_tensor(
                out=bb1[:, n0:n1], in0=pg[pp][0:1, 0:n1 - n0], scalar=1.0,
                in1=b1_sb[:, n0:n1], op0=ALU.mult, op1=ALU.add),
                reads=[f"pg{pp}", "b1_sb"], writes=["bb1"])
        for c in range(8):
            sch.op("pool", lambda e, c=c: e.tensor_scalar(
                out=w1_sb[:, c, :], in0=w1_sb[:, c, :],
                scalar1=lng_sb[:, c:c + 1], scalar2=1.0,
                op0=ALU.mult, op1=ALU.mult),
                reads=["w1_sb", "lng_sb"], writes=["w1_sb"])

        NSL = 2
        xsh = [sbB(f"xsh{i}", [128, 1024]) for i in range(NSL)]
        xun = [sbB(f"xun{i}", [128, 1024]) for i in range(NSL)]
        stB = [sbB(f"stB{i}", [128, 8]) for i in range(NSL)]
        junkB = sbB("junkB", [128, 1024])
        xnB = [sbB(f"xnB{i}", [128, 1024], BF16) for i in range(NSL)]
        pbfB = [esB.enter_context(nc.psum_tensor(f"pbfB{i}", [128, 1024], BF16))
                for i in range(2)]
        pbfBn = ["pbfB0", "pbfB1"]
        xnfB = [sbB(f"xnfB{i}", [128, 1024], BF16) for i in range(NSL)]
        gl = sbB("gl", [128, 512], BF16)
        ag = [sbB("ag0", [128, 2816], BF16)] * NSL
        sch.op("pool", lambda e: e.memset(ag[0][:, 2730:2816], 0.0),
               writes=["ag0"])
        agf = [sbB("agf0", [128, 2816], BF16)] * NSL

        for ot in range(nq):
            s = ot % NSL
            T = lambda n: f"{n}{s}"
            pt = ot + 1
            base = pt * 128
            sch.op("dsp", lambda e, base=base, s=s: e.dma_start(
                out=xsh[s][:, 0:512], in_=xprime[base:base + 128, 0:512]),
                writes=[T("xsh")])
            sch.op("dsp", lambda e, base=base, s=s: e.dma_start(
                out=xsh[s][:, 512:1024],
                in_=xprime[base - 1:base + 127, 512:1024]),
                writes=[T("xsh")])
            sch.op("dsp", lambda e, base=base, s=s: e.dma_start(
                out=xun[s][:], in_=xprime[base:base + 128, :]),
                writes=[T("xun")])
            ln_stats(sch, xsh[s][:], T("xsh"), stB[s], T("stB"), junkB[:],
                     "junkB", consts)
            ln_apply(sch, xsh[s][:], T("xsh"), stB[s], T("stB"), xnB[s][:],
                     T("xnB"))
            transpose_plane(sch, xnB[s][:], T("xnB"), pbfB[s], pbfBn[s],
                            xnfB[s][:], T("xnfB"), ident, 8)
            for j in range(6):
                a0 = j * 455
                mm_chain(sch, pg[0][:, 0:455], "pg0", xnfB[s][:], T("xnfB"),
                         lambda c, a0=a0: w1_sb[:, c, a0:a0 + 455], "w1_sb", 8,
                         extra=[(onesb[:], bb1[:, a0:a0 + 455])],
                         extra_reads=["onesb", "bb1"])
                mm_chain(sch, pg[1][:, 0:455], "pg1", xnfB[s][:], T("xnfB"),
                         lambda c, a0=a0: w1_sb[:, c, 2730 + a0:2730 + a0 + 455],
                         "w1_sb", 8,
                         extra=[(onesb[:], bb1[:, 2730 + a0:2730 + a0 + 455])],
                         extra_reads=["onesb", "bb1"])
                sch.op("act", lambda e: e.activation(
                    out=gl[:, 0:455], in_=pg[1][:, 0:455], func=AF.Gelu,
                    bias=consts.col(0.0), scale=1.0),
                    reads=["pg1", "consts"], writes=["gl"])
                sch.op("dve", lambda e, a0=a0, s=s: e.tensor_tensor(
                    out=ag[s][:, a0:a0 + 455], in0=pg[0][:, 0:455],
                    in1=gl[:, 0:455], op=ALU.mult),
                    reads=["pg0", "gl"], writes=["ag0"])
            for g in range(3):
                nch = 8 if g < 2 else 6
                pp = g % 2
                for c in range(nch):
                    sch.op("pe", lambda e, g=g, c=c, pp=pp, s=s: e.transpose(
                        pbfB[pp][:, c * 128:(c + 1) * 128],
                        ag[s][:, (g * 8 + c) * 128:(g * 8 + c + 1) * 128],
                        ident[:]), reads=["ag0", "ident"], writes=[pbfBn[pp]])
                sch.op("dve", lambda e, g=g, nch=nch, pp=pp, s=s: e.tensor_copy(
                    agf[s][:, g * 1024:g * 1024 + nch * 128],
                    pbfB[pp][:, 0:nch * 128]),
                    reads=[pbfBn[pp]], writes=["agf0"])
            for oc in range(2):
                mm_chain(sch, pg[oc][:], f"pg{oc}", agf[s][:], "agf0",
                         lambda c, oc=oc: w2_sb[:, c, oc * 512:(oc + 1) * 512],
                         "w2_sb", 22,
                         extra=[(onesb[:], b2_sb[:, oc * 512:(oc + 1) * 512])],
                         extra_reads=["onesb", "b2_sb"])
                sch.op("dve", lambda e, oc=oc, s=s: e.scalar_tensor_tensor(
                    out=xun[s][:, oc * 512:(oc + 1) * 512], in0=pg[oc][:],
                    scalar=1.0, in1=xun[s][:, oc * 512:(oc + 1) * 512],
                    op0=ALU.mult, op1=ALU.add),
                    reads=[f"pg{oc}"], writes=[T("xun")])
            sch.op("dsp", lambda e, ot=ot, s=s: e.dma_start(
                out=xout_ap[ot * 128:(ot + 1) * 128, :], in_=xun[s][:]),
                reads=[T("xun")], writes=[T("xun") + "_st"])


def make_l0ffn_program(nq=NT, pre_ln=False):
    nc = bass.Bass("TRN2", target_bir_lowering=False)
    ngt = nq + HALO_T
    xh = nc.dram_tensor("xh", [ngt * 128, D], F32, kind="ExternalInput").ap()
    wqkv = nc.dram_tensor("wqkv", [1024, 3072], F32, kind="ExternalInput").ap()
    qs = nc.dram_tensor("qs", [64], F32, kind="ExternalInput").ap()
    ks = nc.dram_tensor("ks", [64], F32, kind="ExternalInput").ap()
    wo = nc.dram_tensor("wo", [1024, 1024], F32, kind="ExternalInput").ap()
    lng = nc.dram_tensor("lng", [128, 8], F32, kind="ExternalInput").ap()
    lnb = nc.dram_tensor("lnb", [128, 8], F32, kind="ExternalInput").ap()
    w1 = nc.dram_tensor("w1", [1024, 5460], F32, kind="ExternalInput").ap()
    b1 = nc.dram_tensor("b1", [5460], F32, kind="ExternalInput").ap()
    w2 = nc.dram_tensor("w2", [2816, 1024], F32, kind="ExternalInput").ap()
    b2 = nc.dram_tensor("b2", [1024], F32, kind="ExternalInput").ap()
    kb = nc.dram_tensor("kb", [128, ngt], F32, kind="ExternalInput").ap()
    pos0 = nc.dram_tensor("pos0", [1, 1], F32, kind="ExternalInput").ap()
    xout = nc.dram_tensor("xout", [nq * 128, D], F32, kind="ExternalOutput").ap()
    es = ExitStack()
    with es:
        sch = Sched()
        build_l0ffn_depth(sch, nc, es, xh, wqkv, qs, ks, wo, lng, lnb, w1, b1,
                          w2, b2, kb, pos0, xout, nq=nq, pre_ln=pre_ln)
        sch.lower(nc, es)
    return nc


def make_ssm_proj_program():
    nc = bass.Bass("TRN2", target_bir_lowering=False)
    xin = nc.dram_tensor("xin", [TPC, D], F32, kind="ExternalInput").ap()
    z = nc.dram_tensor("z", [TPC, D], F32, kind="ExternalInput").ap()
    win = nc.dram_tensor("win", [D, 2048], F32, kind="ExternalInput").ap()
    bin_ = nc.dram_tensor("bin", [2048], F32, kind="ExternalInput").ap()
    wg = nc.dram_tensor("wg", [D, 2048], F32, kind="ExternalInput").ap()
    bg = nc.dram_tensor("bg", [2048], F32, kind="ExternalInput").ap()
    xp = nc.dram_tensor("xp", [TPC, 2048], F32, kind="ExternalOutput").ap()
    gt = nc.dram_tensor("gt", [TPC, 2048], F32, kind="ExternalOutput").ap()
    x3 = nc.dram_tensor("x3", [TPC, D], F32, kind="ExternalOutput").ap()
    es = ExitStack()
    with es:
        sch = Sched()
        build_ssm_proj(sch, nc, es, xin, z, win, bin_, wg, bg, xp, gt, x3,
                       ntiles=NT)
        sch.lower(nc, es)
    return nc


def make_ssm_scan_program():
    nc = bass.Bass("TRN2", target_bir_lowering=False)
    xcs = nc.dram_tensor("xcs", [512, 47 + L], F32, kind="ExternalInput").ap()
    A = nc.dram_tensor("A", [256, 16, 16], F32, kind="ExternalInput").ap()
    Bt = nc.dram_tensor("B", [256, 16], F32, kind="ExternalInput").ap()
    C = nc.dram_tensor("C", [256, 16], F32, kind="ExternalInput").ap()
    ycs = nc.dram_tensor("ycs", [512, L], F32, kind="ExternalOutput").ap()
    es = ExitStack()
    with es:
        sch = Sched()
        build_ssm_scan(sch, nc, es, xcs, A, Bt, C, ycs, nrows=512, Ltime=L,
                       nch=256)
        sch.lower(nc, es)
    return nc


def make_ssm_out_program():
    nc = bass.Bass("TRN2", target_bir_lowering=False)
    xin = nc.dram_tensor("xin", [TPC, D], F32, kind="ExternalInput").ap()
    y = nc.dram_tensor("y", [TPC, 2048], F32, kind="ExternalInput").ap()
    g = nc.dram_tensor("g", [TPC, 2048], F32, kind="ExternalInput").ap()
    wo = nc.dram_tensor("wo", [2048, D], F32, kind="ExternalInput").ap()
    bo = nc.dram_tensor("bo", [D], F32, kind="ExternalInput").ap()
    xo = nc.dram_tensor("xo", [TPC, D], F32, kind="ExternalOutput").ap()
    es = ExitStack()
    with es:
        sch = Sched()
        build_ssm_out(sch, nc, es, xin, y, g, wo, bo, xo, ntiles=NT)
        sch.lower(nc, es)
    return nc


_prog_cache = {}


def _prog(name, fn):
    if name not in _prog_cache:
        _prog_cache[name] = fn()
    return _prog_cache[name]


def _core_slice(x2d, halo):
    out = []
    for c in range(NCORES):
        b, o = c // 4, (c % 4) * TPC
        g0 = b * L + o
        buf = np.zeros((halo + TPC, x2d.shape[1]), x2d.dtype)
        h0 = min(halo, o)
        if h0 > 0:
            buf[halo - h0:halo] = x2d[g0 - h0:g0]
        buf[halo:] = x2d[g0:g0 + TPC]
        out.append((buf, float(o - halo), o == 0))
    return out


def kernel(L0_attn_x, L0_attn_w_qkv, L0_attn_w_out, L0_ffn_Wqkv, L0_ffn_q_scale,
           L0_ffn_k_scale, L0_ffn_Wo, L0_ffn_ln_g, L0_ffn_ln_b, L0_ffn_W1,
           L0_ffn_b1, L0_ffn_W2, L0_ffn_b2, L1_attn_w_in, L1_attn_b_in,
           L1_attn_w_gate, L1_attn_b_gate, L1_attn_A, L1_attn_B_ssm,
           L1_attn_C_ssm, L1_attn_w_out, L1_attn_b_out, L1_ffn_W_fc,
           L1_ffn_W_proj):
    f32 = np.float32
    x = np.ascontiguousarray(np.asarray(L0_attn_x, f32).reshape(B * L, D))
    cores = list(range(NCORES))

    nc = _prog("l0attn", lambda: make_l0attn_program(nq=NT))
    maps = []
    for (buf, pos0, sstart) in _core_slice(x, 512):
        kb = np.full((512,), -1e30, f32) if sstart else np.zeros((512,), f32)
        maps.append({"xh": buf, "wqkv": np.asarray(L0_attn_w_qkv, f32),
                     "wout": np.asarray(L0_attn_w_out, f32),
                     "kb": kb.reshape(4, 128).T.copy(),
                     "pos0": np.array([[pos0]], f32)})
    res = run_bass_kernel_spmd(nc, maps, core_ids=cores)
    x = np.concatenate([r["xout"] for r in res.results], 0)

    x1 = x
    HH = HALO_T * 128
    ngt = NT + HALO_T
    for i in range(6):
        nc = (_prog("l0ffn_pre", lambda: make_l0ffn_program(nq=NT, pre_ln=True))
              if i == 0 else _prog("l0ffn", lambda: make_l0ffn_program(nq=NT)))
        w2pad = np.concatenate([np.asarray(L0_ffn_W2[i], f32),
                                np.zeros((2816 - 2730, D), f32)], 0)
        maps = []
        for (buf, pos0, sstart) in _core_slice(x, HH):
            kb = np.zeros((ngt * 128,), f32)
            if sstart:
                kb[:HH] = -1e30
            maps.append({
                "xh": buf, "wqkv": np.asarray(L0_ffn_Wqkv[i], f32),
                "qs": np.asarray(L0_ffn_q_scale[i], f32),
                "ks": np.asarray(L0_ffn_k_scale[i], f32),
                "wo": np.asarray(L0_ffn_Wo[i], f32),
                "lng": np.asarray(L0_ffn_ln_g[i], f32).reshape(8, 128).T.copy(),
                "lnb": np.asarray(L0_ffn_ln_b[i], f32).reshape(8, 128).T.copy(),
                "w1": np.asarray(L0_ffn_W1[i], f32),
                "b1": np.asarray(L0_ffn_b1[i], f32), "w2": w2pad,
                "b2": np.asarray(L0_ffn_b2[i], f32),
                "kb": kb.reshape(ngt, 128).T.copy(),
                "pos0": np.array([[pos0]], f32)})
        res = run_bass_kernel_spmd(nc, maps, core_ids=cores)
        x = np.concatenate([r["xout"] for r in res.results], 0)

    nc = _prog("ssm1", make_ssm_proj_program)
    maps = [{"xin": x1[c * TPC:(c + 1) * TPC], "z": x[c * TPC:(c + 1) * TPC],
             "win": np.asarray(L1_attn_w_in, f32),
             "bin": np.asarray(L1_attn_b_in, f32),
             "wg": np.asarray(L1_attn_w_gate, f32),
             "bg": np.asarray(L1_attn_b_gate, f32)} for c in cores]
    res = run_bass_kernel_spmd(nc, maps, core_ids=cores)
    xp = np.concatenate([r["xp"] for r in res.results], 0)
    gt = np.concatenate([r["gt"] for r in res.results], 0)
    x = np.concatenate([r["x3"] for r in res.results], 0)

    xp3 = xp.reshape(B, L, 2048)
    nc = _prog("ssm2", make_ssm_scan_program)
    maps = []
    for c in cores:
        ch0 = c * 256
        xcs = np.zeros((512, 47 + L), f32)
        for b in range(B):
            xcs[b * 256:(b + 1) * 256, 47:] = xp3[b, :, ch0:ch0 + 256].T
        maps.append({"xcs": xcs,
                     "A": np.asarray(L1_attn_A[ch0:ch0 + 256], f32),
                     "B": np.asarray(L1_attn_B_ssm[ch0:ch0 + 256], f32),
                     "C": np.asarray(L1_attn_C_ssm[ch0:ch0 + 256], f32)})
    res = run_bass_kernel_spmd(nc, maps, core_ids=cores)
    y3 = np.zeros((B, L, 2048), f32)
    for c in cores:
        ch0 = c * 256
        ycs = res.results[c]["ycs"]
        for b in range(B):
            y3[b, :, ch0:ch0 + 256] = ycs[b * 256:(b + 1) * 256].T
    yt = y3.reshape(B * L, 2048)

    nc = _prog("ssm3", make_ssm_out_program)
    maps = [{"xin": x[c * TPC:(c + 1) * TPC], "y": yt[c * TPC:(c + 1) * TPC],
             "g": gt[c * TPC:(c + 1) * TPC],
             "wo": np.asarray(L1_attn_w_out, f32),
             "bo": np.asarray(L1_attn_b_out, f32)} for c in cores]
    res = run_bass_kernel_spmd(nc, maps, core_ids=cores)
    x = np.concatenate([r["xo"] for r in res.results], 0)

    nc = _prog("l1ffn", make_l1ffn_program)
    maps = [{"xin": x[c * TPC:(c + 1) * TPC],
             "wfc": np.asarray(L1_ffn_W_fc, f32),
             "wproj": np.asarray(L1_ffn_W_proj, f32)} for c in cores]
    res = run_bass_kernel_spmd(nc, maps, core_ids=cores)
    x = np.concatenate([r["xout"] for r in res.results], 0)
    return x.reshape(B, L, D)
